# version 11
# baseline (speedup 1.0000x reference)
import numpy as np
from contextlib import ExitStack

import concourse.bass as bass
import concourse.mybir as mybir
from concourse.bass_utils import run_bass_kernel_spmd

F32 = mybir.dt.float32
BF16 = mybir.dt.bfloat16
I32 = mybir.dt.int32
ACT = mybir.ActivationFunctionType
ALU = mybir.AluOpType

NCORES = 8
TOK = 4096
C = 1024
FF = 4096
QC = 512
NQC = TOK // QC
FC = 256
NFC = TOK // FC
NCB = C // 128
NFB = FF // 128

K_ELEMS = 64 * TOK
V_ELEMS = 128 * 2080
W1_ELEMS = 4 * 1024 * 128
W2_ELEMS = 512 * 1024
BK0 = 0
BV0 = K_ELEMS
BW10 = BV0 + V_ELEMS
BW20 = BW10 + W1_ELEMS
BN_ELEMS = BW20 + W2_ELEMS

DG_W_DONE = 19
DG_BOUNCED = 21
DG_READY = 45


def build_nc(p2=True, p3n=NQC, p1n=NFC, p1lvl=67, dbg=False):
    nc = bass.Bass("TRN2", target_bir_lowering=False, detect_race_conditions=False)

    xD = nc.dram_tensor("x", [TOK, C], F32, kind="ExternalInput")
    wqkT = nc.dram_tensor("wqkT", [C, 128], F32, kind="ExternalInput")
    wqqT = nc.dram_tensor("wqqT", [C, 128], F32, kind="ExternalInput")
    wvT = nc.dram_tensor("wvT", [C, 64], F32, kind="ExternalInput")
    bqkD = nc.dram_tensor("bqk", [128], F32, kind="ExternalInput")
    bq2D = nc.dram_tensor("bq2", [128], F32, kind="ExternalInput")
    bvD = nc.dram_tensor("bv", [64], F32, kind="ExternalInput")
    w1sD = nc.dram_tensor("w1s", [C, 512], F32, kind="ExternalInput")
    w2sD = nc.dram_tensor("w2s", [512, C], F32, kind="ExternalInput")
    b1D = nc.dram_tensor("b1r", [128, 32], F32, kind="ExternalInput")
    b2D = nc.dram_tensor("b2", [C], F32, kind="ExternalInput")
    idD = nc.dram_tensor("idf", [128, 128], F32, kind="ExternalInput")
    outD = nc.dram_tensor("out", [TOK, C], F32, kind="ExternalOutput")
    dbgD = nc.dram_tensor("dbg", [128, 4224], F32, kind="ExternalOutput") if dbg else None

    xntD = nc.dram_tensor("xnt_hbm", [128, TOK * NCB], BF16)
    bncD = nc.dram_tensor("bounce", [BN_ELEMS], BF16)
    gatD = nc.dram_tensor("gather", [NCORES * BN_ELEMS], BF16, addr_space="Shared")

    es = ExitStack()
    S = lambda name, shp, dt: es.enter_context(nc.sbuf_tensor(name, shp, dt))
    P = lambda name, shp, dt: es.enter_context(nc.psum_tensor(name, shp, dt))
    SEM = lambda name: es.enter_context(nc.semaphore(name))

    k2 = S("k2", [128, 16384], BF16)
    vaug = S("vaug", [128, 16640], BF16)
    q2 = S("q2", [128, TOK], BF16)
    qk = S("qk", [128, TOK], BF16)
    xnt = S("xnt", [128, 2, NCB, 256], BF16)
    xbuf = S("xbuf", [128, 2, 2, 1024], F32)
    xnf = S("xnf", [128, 2, 1024], F32)
    wq_bf = S("wq_bf", [128, 1024], BF16)
    wqq_bf = S("wqq_bf", [128, 1024], BF16)
    wv_bf = S("wv_bf", [128, 512], BF16)
    idf = S("idf_s", [128, 128], F32)
    bqk_s = S("bqk_s", [128, 1], F32)
    bq2_s = S("bq2_s", [128, 1], F32)
    c_eps = S("c_eps", [128, 1], F32)
    c_q = S("c_q", [128, 1], F32)
    b1s = S("b1s", [128, NFB], F32)
    b2b = S("b2b", [128, 1024], F32)
    bvb = S("bvb", [128, 64], F32)
    w1r = S("w1r", [128, 4, NCB, 128], BF16)
    w2r = S("w2r", [128, 4, 1024], BF16)
    at = S("at", [128, 2, 1024], BF16)
    yts = S("yts", [128, 512], F32)
    ytmp = S("ytmp", [128, 512], F32)
    rcp = S("rcp", [128, 1], F32)
    ybuf = S("ybuf", [128, 2, 256], F32)
    y1b = S("y1b", [128, 256], F32)
    x0b = S("x0b", [128, 256], I32)
    tpb = S("tpb", [128, 4, 256], I32)
    b0b = S("b0b", [128, 4, 256], I32)
    g0b = S("g0b", [128, 2, 256], F32)
    g1b = S("g1b", [128, 2, 256], F32)
    difb = S("difb", [128, 4, 256], F32)
    z1b = S("z1b", [128, 256], F32)
    g2b = S("g2b", [128, 256], I32)
    gbfb = S("gbfb", [128, 4, 256], BF16)
    z3b = S("z3b", [128, 512], F32)
    x3b = S("x3b", [128, 512], I32)
    of32 = S("of32", [128, 2, 1024], F32)
    obuf = S("obuf", [128, 1024], F32)
    s1 = S("s1", [128, 2, 2], F32)
    s2 = S("s2", [128, 2, 2], F32)
    mbuf = S("mbuf", [128, 2, 2], F32)
    varb = S("varb", [128, 2, 2], F32)
    rb = S("rb", [128, 2, 2], F32)
    mbb = S("mbb", [128, 2, 2], F32)
    msq = S("msq", [128, 2, 2], F32)
    lnb = S("lnb", [128, 2, 2], F32)
    vown = S("vown", [128, 2080], BF16)

    ps_s0 = P("ps_s0", [128, 1024], F32)
    ps_s1 = P("ps_s1", [128, 1024], F32)
    ps_a0 = P("ps_a0", [128, 512], F32)
    ps_a1 = P("ps_a1", [128, 512], F32)
    ps_c = P("ps_c", [128, 1024], F32)

    dx = [SEM("dx0"), SEM("dx1")]
    dxns = [SEM("dxns0"), SEM("dxns1")]
    dxnl = [SEM("dxnl0"), SEM("dxnl1")]
    dw1 = [SEM("dw10"), SEM("dw11"), SEM("dw12"), SEM("dw13")]
    dw2 = SEM("dw2")
    dg = SEM("dg")
    dout = [SEM("dout0"), SEM("dout1")]
    cc = SEM("cc")
    ve_c = SEM("ve_c")
    ve_w = SEM("ve_w")
    ve_st = SEM("ve_st")
    sc_r = SEM("sc_r")
    ve_mb = SEM("ve_mb")
    sc_n = SEM("sc_n")
    mm_tr = SEM("mm_tr")
    ve_xn = SEM("ve_xn")
    mm_qkv = SEM("mm_qkv")
    sc_qk = SEM("sc_qk")
    mm_s = SEM("mm_s")
    exp_c = SEM("exp_c")
    mm_avt = SEM("mm_avt")
    ve_av = SEM("ve_av")
    mm_y = SEM("mm_y")
    ve_y = SEM("ve_y")
    mm_h = SEM("mm_h")
    ve_y1 = SEM("ve_y1")
    ve_pw = SEM("ve_pw")
    sc_g = SEM("sc_g")
    gp_c = SEM("gp_c")
    ve_g = SEM("ve_g")
    mm_o2 = SEM("mm_o2")
    ve_asm = SEM("ve_asm")
    ve_tk = SEM("ve_tk")

    with es:
        with nc.Block() as block:

            @block.sync
            def _(sync):
                for tc in range(p1n):
                    if tc < 2:
                        sync.wait_ge(ve_c, 3)
                    if tc >= 2:
                        if p1lvl >= 4:
                            sync.wait_ge(sc_n, tc - 1)
                        if p1lvl >= 2:
                            sync.wait_ge(ve_st, tc - 1)
                    src = bass.AP(xD, tc * FC * C, [[C, 128], [128 * C, 2], [1, C]])
                    sync.dma_start(out=xbuf[:, tc % 2, :, :], in_=src).then_inc(dx[tc % 2], 16)
                    if p1lvl >= 66:
                        sync.wait_ge(ve_xn, tc + 1)
                        dst = bass.AP(xntD, tc * 256, [[TOK * NCB, 128], [TOK, NCB], [1, 256]])
                        sync.dma_start(out=dst, in_=xnt[:, tc % 2, :, :]).then_inc(dxns[tc % 2], 16)
                for fc in range(2 * p3n):
                    if fc == 0:
                        sync.wait_ge(dxns[0], 16 * (NFC // 2))
                        sync.wait_ge(dxns[1], 16 * (NFC // 2))
                        sync.wait_ge(cc, 1)
                    if fc >= 2:
                        sync.wait_ge(mm_h, NFB * (fc - 1))
                    src = bass.AP(xntD, fc * 256, [[TOK * NCB, 128], [TOK, NCB], [1, 256]])
                    sync.dma_start(out=xnt[:, fc % 2, :, :], in_=src).then_inc(dxnl[fc % 2], 16)
                    if fc >= 2:
                        sync.wait_ge(ve_asm, 2 * (fc - 2) + 2)
                    src = bass.AP(xD, fc * FC * C, [[C, 128], [128 * C, 2], [1, C]])
                    sync.dma_start(out=xbuf[:, fc % 2, :, :], in_=src).then_inc(dx[fc % 2], 16)
                    for g in range(16):
                        if fc > 0 or g >= 2:
                            if g == 0:
                                sync.wait_ge(mm_h, NFB * (fc - 1) + 30)
                            elif g == 1:
                                sync.wait_ge(mm_h, NFB * fc)
                            else:
                                sync.wait_ge(mm_h, NFB * fc + 2 * g - 2)
                        for m in range(2):
                            f = 2 * g + m
                            rk = f // 4
                            lf = f % 4
                            off = rk * BN_ELEMS + BW10 + lf * (1024 * 128)
                            src = bass.AP(gatD, off, [[128, 128], [128 * 128, NCB], [1, 128]])
                            sync.dma_start(out=w1r[:, f % 4, :, :], in_=src).then_inc(dw1[f % 4], 16)
                    if fc >= 1:
                        for t in range(2):
                            sync.wait_ge(ve_asm, 2 * (fc - 1) + t + 1)
                            dst = bass.AP(outD, ((fc - 1) * FC + t * 128) * C, [[C, 128], [1, C]])
                            sync.dma_start(out=dst, in_=of32[:, t, :]).then_inc(dout[t], 16)
                if p3n > 0:
                    for t in range(2):
                        sync.wait_ge(ve_asm, 2 * (2 * p3n - 1) + t + 1)
                        dst = bass.AP(outD, ((2 * p3n - 1) * FC + t * 128) * C,
                                      [[C, 128], [1, C]])
                        sync.dma_start(out=dst, in_=of32[:, t, :]).then_inc(dout[t], 16)

            @block.gpsimd
            def _(g):
                gd = [0]

                def gdma(dst, src, drain=False):
                    g.dma_start(out=dst, in_=src).then_inc(dg, 16)
                    gd[0] += 1
                    if drain:
                        g.wait_ge(dg, 16 * gd[0])

                gdma(xbuf[:, 0, 0, :], bass.AP(wqkT, 0, [[128, 128], [128 * 128, NCB], [1, 128]]), drain=True)
                gdma(xbuf[:, 0, 1, :], bass.AP(wqqT, 0, [[128, 128], [128 * 128, NCB], [1, 128]]), drain=True)
                gdma(xbuf[:, 1, 0, 0:512], bass.AP(wvT, 0, [[64, 128], [64 * 128, NCB], [1, 64]]), drain=True)
                gdma(idf[:, :], idD[:, :], drain=True)
                gdma(bqk_s[:, :], bass.AP(bqkD, 0, [[1, 128], [1, 1]]), drain=True)
                gdma(bq2_s[:, :], bass.AP(bq2D, 0, [[1, 128], [1, 1]]), drain=True)
                gdma(bvb[:, :], bass.AP(bvD, 0, [[0, 128], [1, 64]]), drain=True)
                gdma(b1s[:, :], b1D[:, :], drain=True)
                gdma(b2b[:, :], bass.AP(b2D, 0, [[0, 128], [1, 1024]]), drain=True)
                of_flat = bass.AP(of32, 0, [[2048, 128], [1, 2048]])
                if not p2:
                    if p1lvl >= 66:
                        g.wait_ge(dxns[0], 16 * ((p1n + 1) // 2))
                        g.wait_ge(dxns[1], 16 * (p1n // 2))
                    elif p1lvl >= 6:
                        g.wait_ge(ve_xn, p1n)
                    if p1lvl >= 67:
                        g.wait_ge(sc_qk, p1n)
                    if p1lvl == 5:
                        g.wait_ge(mm_tr, p1n)
                    if p1lvl == 4:
                        g.wait_ge(sc_n, p1n)
                    if p1lvl == 3:
                        g.wait_ge(ve_mb, p1n)
                    if p1lvl == 2:
                        g.wait_ge(ve_st, p1n)
                    if p1lvl == 1:
                        g.wait_ge(dx[0], 16 * ((p1n + 1) // 2))
                        g.wait_ge(dx[1], 16 * (p1n // 2))
                    return
                g.wait_ge(sc_n, p1n)
                g.wait_ge(ve_st, p1n)
                for rnd in range(2):
                    gdma(of_flat, bass.AP(w1sD, rnd * 256,
                                          [[512, 128], [512 * 128, NCB], [1, 256]]))
                    g.wait_ge(ve_w, rnd + 1)
                    for m in range(2):
                        f_loc = 2 * rnd + m
                        dst = bass.AP(bncD, BW10 + f_loc * (1024 * 128),
                                      [[128, 128], [128 * 128, NCB], [1, 128]])
                        src = bass.AP(at, m * 128, [[2048, 128], [256, NCB], [1, 128]])
                        gdma(dst, src)
                for rnd in range(2):
                    gdma(of_flat, bass.AP(w2sD, rnd * 256 * 1024,
                                          [[1024, 128], [128 * 1024, 2], [1, 1024]]))
                    g.wait_ge(ve_w, 3 + rnd)
                    dst = bass.AP(bncD, BW20 + rnd * 256 * 1024,
                                  [[1024, 128], [128 * 1024, 2], [1, 1024]])
                    gdma(dst, bass.AP(at, 0, [[2048, 128], [1024, 2], [1, 1024]]))
                g.wait_ge(dg, 16 * gd[0])
                g.wait_ge(sc_qk, p1n)
                gdma(bass.AP(bncD, BK0, [[TOK, 64], [1, TOK]]), qk[64:128, :])
                gdma(bass.AP(bncD, BV0, [[2080, 128], [1, 2080]]), vown[:, :])
                assert gd[0] == DG_BOUNCED, gd[0]
                g.wait_ge(dg, 16 * DG_BOUNCED)
                g.collective_compute(
                    "AllGather", ALU.bypass,
                    replica_groups=[list(range(NCORES))],
                    ins=[bncD.ap().opt()], outs=[gatD.ap().opt()],
                ).then_inc(cc, 1)
                g.wait_ge(cc, 1)
                for r in range(NCORES):
                    src = bass.AP(gatD, r * BN_ELEMS + BK0, [[TOK, 64], [256, 16], [1, 128]])
                    gdma(k2[0:64, r * 2048:(r + 1) * 2048], src)
                for r in range(NCORES):
                    src = bass.AP(gatD, r * BN_ELEMS + BK0 + 128,
                                  [[TOK, 64], [256, 16], [1, 128]])
                    gdma(k2[64:128, r * 2048:(r + 1) * 2048], src)
                for r in range(NCORES):
                    src = bass.AP(gatD, r * BN_ELEMS + BV0, [[2080, 128], [1, 2080]])
                    gdma(vaug[:, r * 2080:(r + 1) * 2080], src)
                assert gd[0] == DG_READY, gd[0]
                wd = [0]
                for i in range(p3n):
                    for half in range(2):
                        fc = 2 * i + half
                        fb = NFB * fc
                        for g0_ in range(2):
                            if fc > 0:
                                g.wait_ge(mm_o2, NFB * fc - 2 + 2 * g0_)
                            g.wait_ge(dw2, 16 * wd[0])
                            for m in range(2):
                                f = 2 * g0_ + m
                                off = ((f // 4) * BN_ELEMS + BW20 + (f % 4) * 128 * 1024)
                                src = bass.AP(gatD, off, [[1024, 128], [1, 1024]])
                                g.dma_start(out=w2r[:, f % 4, :], in_=src).then_inc(dw2, 16)
                                wd[0] += 1
                        for f in range(NFB):
                            g.wait_ge(sc_g, fb + f + 1)
                            if fb + f >= 4:
                                g.wait_ge(ve_g, fb + f - 3)
                            g.tensor_tensor(out=difb[:, f % 4, :], in0=g1b[:, f % 2, :],
                                            in1=g0b[:, f % 2, :],
                                            op=ALU.subtract).then_inc(gp_c, 1)
                            if f % 2 == 1:
                                gw = (f + 3) // 2
                                if gw <= 15:
                                    g.wait_ge(mm_o2, fb + 2 * gw - 2)
                                    g.wait_ge(dw2, 16 * wd[0])
                                    for m in range(2):
                                        ff = 2 * gw + m
                                        off = ((ff // 4) * BN_ELEMS + BW20
                                               + (ff % 4) * 128 * 1024)
                                        src = bass.AP(gatD, off, [[1024, 128], [1, 1024]])
                                        g.dma_start(out=w2r[:, ff % 4, :],
                                                    in_=src).then_inc(dw2, 16)
                                        wd[0] += 1
                if p3n > 0:
                    g.wait_ge(dout[0], 16 * 2 * p3n)
                    g.wait_ge(dout[1], 16 * 2 * p3n)
                if dbg:
                    dtaps = [
                        (0, k2[:, 0:512]),
                        (512, xnf[:, 1, 0:512]),
                        (1024, vaug[:, 0:512]),
                        (1536, xnf[:, 0, 0:512]),
                        (2048, q2[:, 0:512]),
                        (2560, xbuf[:, 0, 0, 0:512]),
                        (3072, xbuf[:, 1, 0, 0:512]),
                        (3584, bass.AP(ybuf, 0, [[512, 128], [1, 512]])),
                    ]
                    for off_, ap_ in dtaps:
                        gdma(dbgD[:, off_:off_ + 512], ap_)
                    gdma(dbgD[:, 4096:4100], bass.AP(rb, 0, [[4, 128], [1, 4]]))
                    gdma(dbgD[:, 4100:4104], bass.AP(varb, 0, [[4, 128], [1, 4]]))
                    gdma(dbgD[:, 4104:4108], bass.AP(s1, 0, [[4, 128], [1, 4]]))
                    gdma(dbgD[:, 4108:4112], bass.AP(s2, 0, [[4, 128], [1, 4]]))
                    gdma(dbgD[:, 4112:4116], bass.AP(mbuf, 0, [[4, 128], [1, 4]]))
                    gdma(dbgD[:, 4116:4120], bass.AP(msq, 0, [[4, 128], [1, 4]]))
                    gdma(dbgD[:, 4120:4124], bass.AP(lnb, 0, [[4, 128], [1, 4]]))
                    gdma(dbgD[:, 4124:4128], bass.AP(mbb, 0, [[4, 128], [1, 4]]))
                    g.wait_ge(dg, 16 * gd[0])

            @block.vector
            def _(v):
                tkn = [0]

                def tick(instr):
                    instr.then_inc(ve_tk, 1)
                    tkn[0] += 1
                    v.wait_ge(ve_tk, tkn[0])

                v.wait_ge(dg, 16 * 3)
                v.tensor_copy(out=wq_bf[:, :], in_=xbuf[:, 0, 0, :]).then_inc(ve_c, 1)
                v.tensor_copy(out=wqq_bf[:, :], in_=xbuf[:, 0, 1, :]).then_inc(ve_c, 1)
                v.tensor_copy(out=wv_bf[:, :], in_=xbuf[:, 1, 0, 0:512]).then_inc(ve_c, 1)
                v.wait_ge(dg, 16 * 9)
                v.tensor_scalar(out=b1s[:, :], in0=b1s[:, :], scalar1=256.0, scalar2=None,
                                op0=ALU.mult).then_inc(ve_c, 1)
                v.tensor_scalar(out=b2b[:, :], in0=b2b[:, :], scalar1=256.0, scalar2=None,
                                op0=ALU.mult).then_inc(ve_c, 1)
                v.memset(vown[:, :], 1.0).then_inc(ve_c, 1)
                v.memset(c_q[:, :], 0.25).then_inc(ve_c, 1)
                v.memset(c_eps[:, :], 1e-5).then_inc(ve_c, 1)
                v.memset(yts[:, :], 0.0)
                v.memset(ytmp[:, :], 0.0).then_inc(ve_c, 1)
                for tc in range(p1n):
                    sl = tc % 2
                    if p1lvl < 2:
                        continue
                    v.wait_ge(dx[tc % 2], 16 * (tc // 2 + 1))
                    for t in range(2):
                        v.reduce_sum(out=s1[:, sl, t:t + 1], in_=xbuf[:, sl, t, :],
                                     axis=mybir.AxisListType.X)
                        v.scalar_tensor_tensor(out=of32[:, t, :], in0=xbuf[:, sl, t, :],
                                               scalar=1.0, in1=xbuf[:, sl, t, :],
                                               op0=ALU.mult, op1=ALU.mult,
                                               accum_out=s2[:, sl, t:t + 1])
                    tick(v.tensor_scalar(out=mbuf[:, sl, :], in0=s1[:, sl, :],
                                         scalar1=1.0 / C, scalar2=None, op0=ALU.mult))
                    tick(v.tensor_tensor(out=msq[:, sl, :], in0=mbuf[:, sl, :],
                                         in1=mbuf[:, sl, :], op=ALU.mult))
                    v.scalar_tensor_tensor(out=varb[:, sl, :], in0=s2[:, sl, :],
                                           scalar=1.0 / C, in1=msq[:, sl, :],
                                           op0=ALU.mult, op1=ALU.subtract).then_inc(ve_st, 1)
                    if p1lvl < 3:
                        continue
                    v.wait_ge(sc_r, tc + 1)
                    v.scalar_tensor_tensor(out=mbb[:, sl, :], in0=mbuf[:, sl, :],
                                           scalar=-1.0, in1=rb[:, sl, :],
                                           op0=ALU.mult, op1=ALU.mult).then_inc(ve_mb, 1)
                    if p1lvl < 6:
                        continue
                    v.wait_ge(mm_tr, tc + 1)
                    if tc >= 2:
                        if p1lvl >= 67:
                            v.wait_ge(mm_qkv, tc - 1)
                        if p1lvl >= 66:
                            v.wait_ge(dxns[tc % 2], 16 * ((tc - 2) // 2 + 1))
                    for t in range(2):
                        ps = ps_s0 if t == 0 else ps_s1
                        for cb in range(NCB):
                            cp = v.tensor_copy(out=xnt[:, sl, cb, t * 128:(t + 1) * 128],
                                               in_=ps[:, cb * 128:(cb + 1) * 128])
                    cp.then_inc(ve_xn, 1)
                at_flat = bass.AP(at, 0, [[2048, 128], [1, 2048]])
                of_flat = bass.AP(of32, 0, [[2048, 128], [1, 2048]])
                if not p2:
                    return
                for rnd, thr in enumerate((10, 13, 16, 18)):
                    v.wait_ge(dg, 16 * thr)
                    v.tensor_copy(out=at_flat, in_=of_flat).then_inc(ve_w, 1)
                for i in range(p3n):
                    v.wait_ge(mm_avt, 128 * (i + 1))
                    v.tensor_copy(out=yts[0:65, :], in_=ps_a0[0:65, :]).then_inc(ve_av, 1)
                    v.wait_ge(mm_y, i + 1)
                    if i == 0:
                        v.wait_ge(dg, 16 * 7)
                        v.wait_ge(ve_c, 9)
                    for t in range(4):
                        tick(v.reciprocal(out=rcp[:, :],
                                          in_=ps_c[:, t * 128 + 64:t * 128 + 65]))
                        dv = v.scalar_tensor_tensor(out=ybuf[:, i % 2, t * 64:(t + 1) * 64],
                                                    in0=ps_c[:, t * 128:t * 128 + 64],
                                                    scalar=rcp[:, :], in1=bvb[:, :],
                                                    op0=ALU.mult, op1=ALU.add)
                    dv.then_inc(ve_y, 1)
                    for half in range(2):
                        fc = 2 * i + half
                        fb = NFB * fc
                        for f in range(NFB + 2):
                            if f < NFB:
                                v.wait_ge(mm_h, fb + f + 1)
                                off = (f % 2) * 512
                                v.tensor_scalar(out=y1b[:, :], in0=ps_s1[:, off:off + 256],
                                                scalar1=256.0, scalar2=b1s[:, f:f + 1],
                                                op0=ALU.mult, op1=ALU.add).then_inc(ve_y1, 1)
                                v.tensor_scalar(out=x0b[:, :], in0=y1b[:, :],
                                                scalar1=-1024.0, scalar2=None, op0=ALU.max)
                                v.tensor_scalar(out=tpb[:, f % 4, :], in0=x0b[:, :],
                                                scalar1=63, scalar2=None, op0=ALU.bitwise_and)
                                if fb + f >= 4:
                                    v.wait_ge(sc_g, fb + f - 3)
                                v.tensor_scalar(out=b0b[:, f % 4, :], in0=x0b[:, :],
                                                scalar1=-64, scalar2=None,
                                                op0=ALU.bitwise_and).then_inc(ve_pw, 1)
                            if f >= 2:
                                fz = f - 2
                                v.wait_ge(gp_c, fb + fz + 1)
                                v.scalar_tensor_tensor(out=z1b[:, :], in0=tpb[:, fz % 4, :],
                                                       scalar=4.0, in1=difb[:, fz % 4, :],
                                                       op0=ALU.mult, op1=ALU.mult)
                                v.scalar_tensor_tensor(out=g2b[:, :], in0=g0b[:, fz % 2, :],
                                                       scalar=256.0, in1=z1b[:, :],
                                                       op0=ALU.mult, op1=ALU.add)
                                if fb + fz >= 4:
                                    v.wait_ge(mm_o2, fb + fz - 3)
                                v.tensor_copy(out=gbfb[:, fz % 4, :],
                                              in_=g2b[:, :]).then_inc(ve_g, 1)
                        v.wait_ge(mm_o2, NFB * (fc + 1))
                        if fc == 0:
                            v.wait_ge(ve_c, 5)
                        v.wait_ge(dx[fc % 2], 16 * (NFC // 2 + fc // 2 + 1))
                        for t in range(2):
                            if fc >= 1:
                                v.wait_ge(dout[t], 16 * fc)
                            for hh in range(2):
                                q = 2 * t + hh
                                src = (ps_a0 if q == 0 else ps_a1 if q == 1 else ps_c)
                                srcap = (src[:, 0:512] if q != 3 else src[:, 512:1024])
                                v.tensor_tensor(out=z3b[:, :], in0=srcap,
                                                in1=b2b[:, hh * 512:(hh + 1) * 512], op=ALU.add)
                                v.tensor_scalar(out=x3b[:, :], in0=z3b[:, :], scalar1=1.0,
                                                scalar2=None, op0=ALU.mult)
                                v.scalar_tensor_tensor(
                                    out=obuf[:, hh * 512:(hh + 1) * 512],
                                    in0=x3b[:, :], scalar=1.0 / 256.0,
                                    in1=xbuf[:, fc % 2, t, hh * 512:(hh + 1) * 512],
                                    op0=ALU.mult, op1=ALU.add)
                            t4 = (2 * fc + t) % 4
                            rep = bass.AP(ybuf, (i % 2) * 256 + t4 * 64,
                                          [[512, 128], [0, 16], [1, 64]])
                            v.tensor_tensor(out=of32[:, t, :], in0=obuf[:, :],
                                            in1=rep, op=ALU.add).then_inc(ve_asm, 1)

            def qkv_evac(s, tcq):
                s.wait_ge(mm_qkv, tcq + 1)
                if tcq == 0:
                    s.wait_ge(dg, 16 * 6)
                    s.wait_ge(ve_c, 6)
                s.activation(out=qk[:, tcq * 256:(tcq + 1) * 256], in_=ps_a0[:, 0:256],
                             func=ACT.Identity, bias=bqk_s[:, :], scale=1.0)
                s.activation(out=q2[:, tcq * 256:(tcq + 1) * 256], in_=ps_a1[:, 0:256],
                             func=ACT.Identity, bias=bq2_s[:, :], scale=1.0)
                for t in range(2):
                    col = 65 * (2 * tcq + t)
                    ev = s.activation(out=vown[:, col:col + 64], in_=ps_c[:, t * 64:(t + 1) * 64],
                                      func=ACT.Identity, bias=0.0, scale=1.0)
                ev.then_inc(sc_qk, 1)

            @block.scalar
            def _(s):
                for tc in range(p1n):
                    sl = tc % 2
                    if p1lvl < 3:
                        continue
                    s.wait_ge(ve_st, tc + 1)
                    if tc == 0:
                        s.wait_ge(ve_c, 8)
                    s.activation(out=lnb[:, sl, :], in_=varb[:, sl, :], func=ACT.Ln,
                                 bias=c_eps[:, :], scale=1.0)
                    s.activation(out=rb[:, sl, :], in_=lnb[:, sl, :], func=ACT.Exp,
                                 bias=0.0, scale=-0.5).then_inc(sc_r, 1)
                    if p1lvl < 4:
                        continue
                    s.wait_ge(ve_mb, tc + 1)
                    if tc >= 1 and p1lvl >= 5:
                        s.wait_ge(mm_tr, tc)
                    for t in range(2):
                        nr = s.activation(out=xnf[:, t, :], in_=xbuf[:, sl, t, :],
                                          func=ACT.Identity, bias=mbb[:, sl, t:t + 1],
                                          scale=rb[:, sl, t:t + 1])
                    nr.then_inc(sc_n, 1)
                    if tc >= 1 and p1lvl >= 67:
                        qkv_evac(s, tc - 1)
                if p1n >= 1 and p1lvl >= 67:
                    qkv_evac(s, p1n - 1)
                for i in range(p3n):
                    for pr in range(128):
                        gp = 128 * i + pr
                        s.wait_ge(mm_s, gp + 1)
                        if gp >= 2:
                            s.wait_ge(mm_avt, gp - 1)
                        if gp < 2:
                            s.wait_ge(dg, 16 * DG_W_DONE)
                        reg = ps_s0 if pr % 2 == 0 else ps_s1
                        s.activation(out=at[:, pr % 2, :], in_=reg[:, :], func=ACT.Exp,
                                     bias=0.0, scale=1.0 / 32.0).then_inc(exp_c, 1)
                    for half in range(2):
                        fc = 2 * i + half
                        fb = NFB * fc
                        for f in range(NFB):
                            s.wait_ge(ve_pw, fb + f + 1)
                            if fb + f == 0:
                                s.wait_ge(ve_c, 7)
                            if fb + f >= 2:
                                s.wait_ge(gp_c, fb + f - 1)
                                s.wait_ge(ve_g, fb + f - 1)
                            s.activation(out=g0b[:, f % 2, :], in_=b0b[:, f % 4, :],
                                         func=ACT.Gelu, bias=0.0, scale=1.0 / 256.0)
                            s.activation(out=g1b[:, f % 2, :], in_=b0b[:, f % 4, :],
                                         func=ACT.Gelu, bias=c_q[:, :],
                                         scale=1.0 / 256.0).then_inc(sc_g, 1)

            @block.tensor
            def _(t):
                for tc in range(p1n + 1):
                    if p1lvl < 5:
                        continue
                    if tc < p1n:
                        t.wait_ge(sc_n, tc + 1)
                        if tc == 0:
                            t.wait_ge(dg, 16 * 4)
                        if tc >= 1:
                            t.wait_ge(ve_xn, tc)
                        t.drain()
                        for ti in range(2):
                            ps = ps_s0 if ti == 0 else ps_s1
                            for cb in range(NCB):
                                tr = t.transpose(ps[:, cb * 128:(cb + 1) * 128],
                                                 xnf[:, ti, cb * 128:(cb + 1) * 128], idf[:, :])
                        tr.then_inc(mm_tr, 1)
                    if tc >= 1 and p1lvl >= 67:
                        tcq = tc - 1
                        t.wait_ge(ve_xn, tcq + 1)
                        if tcq == 0:
                            t.wait_ge(ve_c, 3)
                        if tcq >= 1:
                            t.wait_ge(sc_qk, tcq)
                        sl = tcq % 2
                        for cb in range(NCB):
                            t.matmul(ps_a0[:, 0:256], wq_bf[:, cb * 128:(cb + 1) * 128],
                                     xnt[:, sl, cb, :], start=(cb == 0), stop=(cb == NCB - 1))
                        for cb in range(NCB):
                            t.matmul(ps_a1[:, 0:256], wqq_bf[:, cb * 128:(cb + 1) * 128],
                                     xnt[:, sl, cb, :], start=(cb == 0), stop=(cb == NCB - 1))
                        for ti in range(2):
                            for cb in range(NCB):
                                vm = t.matmul(ps_c[:, ti * 64:(ti + 1) * 64],
                                              xnt[:, sl, cb, ti * 128:(ti + 1) * 128],
                                              wv_bf[:, cb * 64:(cb + 1) * 64],
                                              start=(cb == 0), stop=(cb == NCB - 1))
                        vm.then_inc(mm_qkv, 1)
                for i in range(p3n if p2 else 0):
                    t.wait_ge(dg, 16 * DG_READY)
                    if i == 0:
                        t.wait_ge(sc_qk, NFC)
                    if i >= 1:
                        t.wait_ge(ve_asm, 4 * i)
                    for pr in range(129):
                        if pr < 128:
                            gp = 128 * i + pr
                            if gp >= 2:
                                t.wait_ge(exp_c, gp - 1)
                            if pr == 1 and i >= 1:
                                t.wait_ge(ve_y1, NFB * 2 * i)
                            reg = ps_s0 if pr % 2 == 0 else ps_s1
                            t.matmul(reg[:, 0:512], k2[0:64, pr * 128:(pr + 1) * 128],
                                     q2[0:64, i * QC:(i + 1) * QC], start=True, stop=True,
                                     tile_position=(0, 0))
                            t.matmul(reg[:, 512:1024], k2[64:128, pr * 128:(pr + 1) * 128],
                                     q2[64:128, i * QC:(i + 1) * QC], start=True, stop=True,
                                     tile_position=(64, 0)).then_inc(mm_s, 1)
                        if pr >= 1:
                            pa = pr - 1
                            t.wait_ge(exp_c, 128 * i + pa + 1)
                            first = pa == 0
                            last = pa == 127
                            for ko in range(2):
                                e = 2 * pa + ko
                                acol = ko * 512
                                av = t.matmul(ps_a0[0:65, :], vaug[:, 65 * e:65 * e + 65],
                                              at[:, pa % 2, acol:acol + 512],
                                              start=(first and ko == 0),
                                              stop=(last and ko == 1))
                            av.then_inc(mm_avt, 1)
                    t.drain()
                    t.wait_ge(ve_av, i + 1)
                    for ti in range(4):
                        yt = t.transpose(ps_c[:, ti * 128:ti * 128 + 66],
                                         yts[0:66, ti * 128:(ti + 1) * 128], idf[0:66, 0:66])
                    yt.then_inc(mm_y, 1)
                    t.drain()
                    for half in range(2):
                        fc = 2 * i + half
                        fb = NFB * fc
                        for f in range(NFB + 2):
                            if f < NFB:
                                t.wait_ge(dw1[f % 4], 16 * ((fb + f) // 4 + 1))
                                t.wait_ge(dxnl[fc % 2], 16 * (fc // 2 + 1))
                                if fb + f >= 2:
                                    t.wait_ge(ve_y1, fb + f - 1)
                                if f < 2:
                                    t.wait_ge(exp_c, 128 * (i + 1))
                                off = (f % 2) * 512
                                for cb in range(NCB):
                                    hm = t.matmul(ps_s1[:, off:off + 256],
                                                  w1r[:, f % 4, cb, :], xnt[:, fc % 2, cb, :],
                                                  start=(cb == 0), stop=(cb == NCB - 1))
                                hm.then_inc(mm_h, 1)
                            if f >= 2:
                                fz = f - 2
                                t.wait_ge(ve_g, fb + fz + 1)
                                t.wait_ge(dw2, 16 * (fb + 2 * (fz // 2 + 1)))
                                if fz < 2:
                                    if half == 0:
                                        t.wait_ge(ve_y, i + 1)
                                        t.wait_ge(ve_av, i + 1)
                                    else:
                                        t.wait_ge(ve_asm, 4 * i + 2)
                                for ti in range(2):
                                    for hh in range(2):
                                        q = 2 * ti + hh
                                        dst = (ps_a0 if q == 0 else ps_a1 if q == 1 else ps_c)
                                        dstap = (dst[:, 0:512] if q != 3 else dst[:, 512:1024])
                                        om = t.matmul(dstap,
                                                      gbfb[:, fz % 4, ti * 128:(ti + 1) * 128],
                                                      w2r[:, fz % 4, hh * 512:(hh + 1) * 512],
                                                      start=(fz == 0), stop=(fz == NFB - 1))
                                om.then_inc(mm_o2, 1)
    return nc


def kernel(L0_attn_embeddings, L0_attn_Wq, L0_attn_bq, L0_attn_Wk, L0_attn_bk,
           L0_attn_Wv, L0_attn_bv, L0_ffn_w1, L0_ffn_b1, L0_ffn_w2, L0_ffn_b2):
    x = np.ascontiguousarray(np.asarray(L0_attn_embeddings, dtype=np.float32)
                             .reshape(NCORES * TOK, C))
    Wq = np.asarray(L0_attn_Wq, dtype=np.float32)
    Wk = np.asarray(L0_attn_Wk, dtype=np.float32)
    Wv = np.asarray(L0_attn_Wv, dtype=np.float32)
    bq = np.asarray(L0_attn_bq, dtype=np.float32)
    bk = np.asarray(L0_attn_bk, dtype=np.float32)
    bv = np.asarray(L0_attn_bv, dtype=np.float32)
    w1 = np.asarray(L0_ffn_w1, dtype=np.float32)
    b1 = np.asarray(L0_ffn_b1, dtype=np.float32)
    w2 = np.asarray(L0_ffn_w2, dtype=np.float32)
    b2 = np.asarray(L0_ffn_b2, dtype=np.float32)

    wqkT_ = np.ascontiguousarray(np.concatenate([Wq.T, Wk.T], axis=1))
    wqqT_ = np.ascontiguousarray(np.concatenate([Wq.T, Wq.T], axis=1))
    wvT_ = np.ascontiguousarray(Wv.T)
    bqk = np.ascontiguousarray(np.concatenate([bq, bk]))
    bq2 = np.ascontiguousarray(np.concatenate([bq, bq]))
    idf = np.eye(128, dtype=np.float32)

    nc = build_nc()
    in_maps = []
    for r in range(NCORES):
        in_maps.append({
            "x": np.ascontiguousarray(x[r * TOK:(r + 1) * TOK]),
            "wqkT": wqkT_, "wqqT": wqqT_, "wvT": wvT_,
            "bqk": bqk, "bq2": bq2, "bv": bv,
            "w1s": np.ascontiguousarray(w1[:, r * 512:(r + 1) * 512]),
            "w2s": np.ascontiguousarray(w2[r * 512:(r + 1) * 512, :]),
            "b1r": np.ascontiguousarray(b1.reshape(32, 128).T), "b2": b2, "idf": idf,
        })
    res = run_bass_kernel_spmd(nc, in_maps, core_ids=list(range(NCORES)))
    out = np.concatenate([np.asarray(r["out"]) for r in res.results], axis=0)
    return out.reshape(NCORES, TOK, C).astype(np.float32)


# version 12
# speedup vs baseline: 1.0139x; 1.0139x over previous
import numpy as np
from contextlib import ExitStack

import concourse.bass as bass
import concourse.mybir as mybir
from concourse.bass_utils import run_bass_kernel_spmd

F32 = mybir.dt.float32
BF16 = mybir.dt.bfloat16
I32 = mybir.dt.int32
ACT = mybir.ActivationFunctionType
ALU = mybir.AluOpType

NCORES = 8
TOK = 4096
C = 1024
FF = 4096
QC = 512
NQC = TOK // QC
FC = 256
NFC = TOK // FC
NCB = C // 128
NFB = FF // 128

K_ELEMS = 64 * TOK
V_ELEMS = 128 * 2080
W1_ELEMS = 4 * 1024 * 128
W2_ELEMS = 512 * 1024
BK0 = 0
BV0 = K_ELEMS
BW10 = BV0 + V_ELEMS
BW20 = BW10 + W1_ELEMS
BN_ELEMS = BW20 + W2_ELEMS
KVREG = BW10
WREG = BN_ELEMS - BW10
BW20R = BW20 - BW10

DG_W_DONE = 19
DG_BOUNCED = 21
DG_READY = 45


def build_nc(p2=True, p3n=NQC, p1n=NFC, p1lvl=67, dbg=False):
    nc = bass.Bass("TRN2", target_bir_lowering=False, detect_race_conditions=False)

    xD = nc.dram_tensor("x", [TOK, C], F32, kind="ExternalInput")
    wqkT = nc.dram_tensor("wqkT", [C, 128], F32, kind="ExternalInput")
    wqqT = nc.dram_tensor("wqqT", [C, 128], F32, kind="ExternalInput")
    wvT = nc.dram_tensor("wvT", [C, 64], F32, kind="ExternalInput")
    bqkD = nc.dram_tensor("bqk", [128], F32, kind="ExternalInput")
    bq2D = nc.dram_tensor("bq2", [128], F32, kind="ExternalInput")
    bvD = nc.dram_tensor("bv", [64], F32, kind="ExternalInput")
    w1sD = nc.dram_tensor("w1s", [C, 512], F32, kind="ExternalInput")
    w2sD = nc.dram_tensor("w2s", [512, C], F32, kind="ExternalInput")
    b1D = nc.dram_tensor("b1r", [128, 32], F32, kind="ExternalInput")
    b2D = nc.dram_tensor("b2", [C], F32, kind="ExternalInput")
    idD = nc.dram_tensor("idf", [128, 128], F32, kind="ExternalInput")
    outD = nc.dram_tensor("out", [TOK, C], F32, kind="ExternalOutput")
    dbgD = nc.dram_tensor("dbg", [128, 4224], F32, kind="ExternalOutput") if dbg else None

    xntD = nc.dram_tensor("xnt_hbm", [128, TOK * NCB], BF16)
    bncD = nc.dram_tensor("bounce", [BN_ELEMS], BF16)
    gatW = nc.dram_tensor("gatw", [NCORES * WREG], BF16, addr_space="Shared")
    gatKV = nc.dram_tensor("gatkv", [NCORES * KVREG], BF16, addr_space="Shared")

    es = ExitStack()
    S = lambda name, shp, dt: es.enter_context(nc.sbuf_tensor(name, shp, dt))
    P = lambda name, shp, dt: es.enter_context(nc.psum_tensor(name, shp, dt))
    SEM = lambda name: es.enter_context(nc.semaphore(name))

    k2 = S("k2", [128, 16384], BF16)
    vaug = S("vaug", [128, 16640], BF16)
    q2 = S("q2", [128, TOK], BF16)
    qk = S("qk", [128, TOK], BF16)
    xnt = S("xnt", [128, 2, NCB, 256], BF16)
    xbuf = S("xbuf", [128, 2, 2, 1024], F32)
    xnf = S("xnf", [128, 2, 1024], F32)
    wq_bf = S("wq_bf", [128, 1024], BF16)
    wqq_bf = S("wqq_bf", [128, 1024], BF16)
    wv_bf = S("wv_bf", [128, 512], BF16)
    idf = S("idf_s", [128, 128], F32)
    bqk_s = S("bqk_s", [128, 1], F32)
    bq2_s = S("bq2_s", [128, 1], F32)
    c_eps = S("c_eps", [128, 1], F32)
    c_q = S("c_q", [128, 1], F32)
    b1s = S("b1s", [128, NFB], F32)
    b2b = S("b2b", [128, 1024], F32)
    bvb = S("bvb", [128, 64], F32)
    w1r = S("w1r", [128, 4, NCB, 128], BF16)
    w2r = S("w2r", [128, 4, 1024], BF16)
    at = S("at", [128, 2, 1024], BF16)
    yts = S("yts", [128, 512], F32)
    ytmp = S("ytmp", [128, 512], F32)
    rcp = S("rcp", [128, 1], F32)
    ybuf = S("ybuf", [128, 2, 256], F32)
    y1b = S("y1b", [128, 256], F32)
    x0b = S("x0b", [128, 256], I32)
    tpb = S("tpb", [128, 4, 256], I32)
    b0b = S("b0b", [128, 4, 256], I32)
    g0b = S("g0b", [128, 2, 256], F32)
    g1b = S("g1b", [128, 2, 256], F32)
    difb = S("difb", [128, 4, 256], F32)
    z1b = S("z1b", [128, 256], F32)
    g2b = S("g2b", [128, 256], I32)
    gbfb = S("gbfb", [128, 4, 256], BF16)
    z3b = S("z3b", [128, 512], F32)
    x3b = S("x3b", [128, 512], I32)
    of32 = S("of32", [128, 2, 1024], F32)
    obuf = S("obuf", [128, 1024], F32)
    s1 = S("s1", [128, 2, 2], F32)
    s2 = S("s2", [128, 2, 2], F32)
    mbuf = S("mbuf", [128, 2, 2], F32)
    varb = S("varb", [128, 2, 2], F32)
    rb = S("rb", [128, 2, 2], F32)
    mbb = S("mbb", [128, 2, 2], F32)
    msq = S("msq", [128, 2, 2], F32)
    lnb = S("lnb", [128, 2, 2], F32)
    vown = S("vown", [128, 2080], BF16)

    ps_s0 = P("ps_s0", [128, 1024], F32)
    ps_s1 = P("ps_s1", [128, 1024], F32)
    ps_a0 = P("ps_a0", [128, 512], F32)
    ps_a1 = P("ps_a1", [128, 512], F32)
    ps_c = P("ps_c", [128, 1024], F32)

    dx = [SEM("dx0"), SEM("dx1")]
    dxns = [SEM("dxns0"), SEM("dxns1")]
    dxnl = [SEM("dxnl0"), SEM("dxnl1")]
    dw1 = [SEM("dw10"), SEM("dw11"), SEM("dw12"), SEM("dw13")]
    dw2 = SEM("dw2")
    dg = SEM("dg")
    dout = [SEM("dout0"), SEM("dout1")]
    cc = SEM("cc")
    ve_c = SEM("ve_c")
    ve_w = SEM("ve_w")
    ve_st = SEM("ve_st")
    sc_r = SEM("sc_r")
    ve_mb = SEM("ve_mb")
    sc_n = SEM("sc_n")
    mm_tr = SEM("mm_tr")
    ve_xn = SEM("ve_xn")
    mm_qkv = SEM("mm_qkv")
    sc_qk = SEM("sc_qk")
    mm_s = SEM("mm_s")
    exp_c = SEM("exp_c")
    mm_avt = SEM("mm_avt")
    ve_av = SEM("ve_av")
    mm_y = SEM("mm_y")
    ve_y = SEM("ve_y")
    mm_h = SEM("mm_h")
    ve_y1 = SEM("ve_y1")
    ve_pw = SEM("ve_pw")
    sc_g = SEM("sc_g")
    gp_c = SEM("gp_c")
    ve_g = SEM("ve_g")
    mm_o2 = SEM("mm_o2")
    ve_asm = SEM("ve_asm")
    ve_tk = SEM("ve_tk")

    with es:
        with nc.Block() as block:

            @block.sync
            def _(sync):
                for tc in range(p1n):
                    if tc < 2:
                        sync.wait_ge(ve_c, 3)
                    if tc >= 2:
                        if p1lvl >= 4:
                            sync.wait_ge(sc_n, tc - 1)
                        if p1lvl >= 2:
                            sync.wait_ge(ve_st, tc - 1)
                    src = bass.AP(xD, tc * FC * C, [[C, 128], [128 * C, 2], [1, C]])
                    sync.dma_start(out=xbuf[:, tc % 2, :, :], in_=src).then_inc(dx[tc % 2], 16)
                    if p1lvl >= 66:
                        sync.wait_ge(ve_xn, tc + 1)
                        dst = bass.AP(xntD, tc * 256, [[TOK * NCB, 128], [TOK, NCB], [1, 256]])
                        sync.dma_start(out=dst, in_=xnt[:, tc % 2, :, :]).then_inc(dxns[tc % 2], 16)
                for fc in range(2 * p3n):
                    if fc == 0:
                        sync.wait_ge(dxns[0], 16 * (NFC // 2))
                        sync.wait_ge(dxns[1], 16 * (NFC // 2))
                        sync.wait_ge(cc, 1)
                    if fc >= 2:
                        sync.wait_ge(mm_h, NFB * (fc - 1))
                    src = bass.AP(xntD, fc * 256, [[TOK * NCB, 128], [TOK, NCB], [1, 256]])
                    sync.dma_start(out=xnt[:, fc % 2, :, :], in_=src).then_inc(dxnl[fc % 2], 16)
                    if fc >= 2:
                        sync.wait_ge(ve_asm, 2 * (fc - 2) + 2)
                    src = bass.AP(xD, fc * FC * C, [[C, 128], [128 * C, 2], [1, C]])
                    sync.dma_start(out=xbuf[:, fc % 2, :, :], in_=src).then_inc(dx[fc % 2], 16)
                    for g in range(16):
                        if fc > 0 or g >= 2:
                            if g == 0:
                                sync.wait_ge(mm_h, NFB * (fc - 1) + 30)
                            elif g == 1:
                                sync.wait_ge(mm_h, NFB * fc)
                            else:
                                sync.wait_ge(mm_h, NFB * fc + 2 * g - 2)
                        for m in range(2):
                            f = 2 * g + m
                            rk = f // 4
                            lf = f % 4
                            off = rk * WREG + lf * (1024 * 128)
                            src = bass.AP(gatW, off, [[128, 128], [128 * 128, NCB], [1, 128]])
                            sync.dma_start(out=w1r[:, f % 4, :, :], in_=src).then_inc(dw1[f % 4], 16)
                    if fc >= 1:
                        for t in range(2):
                            sync.wait_ge(ve_asm, 2 * (fc - 1) + t + 1)
                            dst = bass.AP(outD, ((fc - 1) * FC + t * 128) * C, [[C, 128], [1, C]])
                            sync.dma_start(out=dst, in_=of32[:, t, :]).then_inc(dout[t], 16)
                if p3n > 0:
                    for t in range(2):
                        sync.wait_ge(ve_asm, 2 * (2 * p3n - 1) + t + 1)
                        dst = bass.AP(outD, ((2 * p3n - 1) * FC + t * 128) * C,
                                      [[C, 128], [1, C]])
                        sync.dma_start(out=dst, in_=of32[:, t, :]).then_inc(dout[t], 16)

            @block.gpsimd
            def _(g):
                gd = [0]

                def gdma(dst, src, drain=False):
                    g.dma_start(out=dst, in_=src).then_inc(dg, 16)
                    gd[0] += 1
                    if drain:
                        g.wait_ge(dg, 16 * gd[0])

                gdma(xbuf[:, 0, 0, :], bass.AP(wqkT, 0, [[128, 128], [128 * 128, NCB], [1, 128]]), drain=True)
                gdma(xbuf[:, 0, 1, :], bass.AP(wqqT, 0, [[128, 128], [128 * 128, NCB], [1, 128]]), drain=True)
                gdma(xbuf[:, 1, 0, 0:512], bass.AP(wvT, 0, [[64, 128], [64 * 128, NCB], [1, 64]]), drain=True)
                gdma(idf[:, :], idD[:, :], drain=True)
                gdma(bqk_s[:, :], bass.AP(bqkD, 0, [[1, 128], [1, 1]]), drain=True)
                gdma(bq2_s[:, :], bass.AP(bq2D, 0, [[1, 128], [1, 1]]), drain=True)
                gdma(bvb[:, :], bass.AP(bvD, 0, [[0, 128], [1, 64]]), drain=True)
                gdma(b1s[:, :], b1D[:, :], drain=True)
                gdma(b2b[:, :], bass.AP(b2D, 0, [[0, 128], [1, 1024]]), drain=True)
                of_flat = bass.AP(of32, 0, [[2048, 128], [1, 2048]])
                if not p2:
                    if p1lvl >= 66:
                        g.wait_ge(dxns[0], 16 * ((p1n + 1) // 2))
                        g.wait_ge(dxns[1], 16 * (p1n // 2))
                    elif p1lvl >= 6:
                        g.wait_ge(ve_xn, p1n)
                    if p1lvl >= 67:
                        g.wait_ge(sc_qk, p1n)
                    if p1lvl == 5:
                        g.wait_ge(mm_tr, p1n)
                    if p1lvl == 4:
                        g.wait_ge(sc_n, p1n)
                    if p1lvl == 3:
                        g.wait_ge(ve_mb, p1n)
                    if p1lvl == 2:
                        g.wait_ge(ve_st, p1n)
                    if p1lvl == 1:
                        g.wait_ge(dx[0], 16 * ((p1n + 1) // 2))
                        g.wait_ge(dx[1], 16 * (p1n // 2))
                    return
                for rnd in range(2):
                    gdma(of_flat, bass.AP(w1sD, rnd * 256,
                                          [[512, 128], [512 * 128, NCB], [1, 256]]))
                    g.wait_ge(ve_w, rnd + 1)
                    for m in range(2):
                        f_loc = 2 * rnd + m
                        dst = bass.AP(bncD, BW10 + f_loc * (1024 * 128),
                                      [[128, 128], [128 * 128, NCB], [1, 128]])
                        src = bass.AP(at, m * 128, [[2048, 128], [256, NCB], [1, 128]])
                        gdma(dst, src)
                for rnd in range(2):
                    gdma(of_flat, bass.AP(w2sD, rnd * 256 * 1024,
                                          [[1024, 128], [128 * 1024, 2], [1, 1024]]))
                    g.wait_ge(ve_w, 3 + rnd)
                    dst = bass.AP(bncD, BW20 + rnd * 256 * 1024,
                                  [[1024, 128], [128 * 1024, 2], [1, 1024]])
                    gdma(dst, bass.AP(at, 0, [[2048, 128], [1024, 2], [1, 1024]]))
                g.wait_ge(dg, 16 * gd[0])
                g.collective_compute(
                    "AllGather", ALU.bypass,
                    replica_groups=[list(range(NCORES))],
                    ins=[bass.AP(bncD, BW10, [[1, WREG]]).opt()],
                    outs=[gatW.ap().opt()],
                ).then_inc(cc, 1)
                g.wait_ge(sc_qk, p1n)
                gdma(bass.AP(bncD, BK0, [[TOK, 64], [1, TOK]]), qk[64:128, :])
                gdma(bass.AP(bncD, BV0, [[2080, 128], [1, 2080]]), vown[:, :])
                assert gd[0] == DG_BOUNCED, gd[0]
                g.wait_ge(dg, 16 * DG_BOUNCED)
                g.collective_compute(
                    "AllGather", ALU.bypass,
                    replica_groups=[list(range(NCORES))],
                    ins=[bass.AP(bncD, BK0, [[1, KVREG]]).opt()],
                    outs=[gatKV.ap().opt()],
                ).then_inc(cc, 1)
                g.wait_ge(cc, 2)
                for r in range(NCORES):
                    src = bass.AP(gatKV, r * KVREG + BK0, [[TOK, 64], [256, 16], [1, 128]])
                    gdma(k2[0:64, r * 2048:(r + 1) * 2048], src)
                for r in range(NCORES):
                    src = bass.AP(gatKV, r * KVREG + BK0 + 128,
                                  [[TOK, 64], [256, 16], [1, 128]])
                    gdma(k2[64:128, r * 2048:(r + 1) * 2048], src)
                for r in range(NCORES):
                    src = bass.AP(gatKV, r * KVREG + BV0, [[2080, 128], [1, 2080]])
                    gdma(vaug[:, r * 2080:(r + 1) * 2080], src)
                assert gd[0] == DG_READY, gd[0]
                wd = [0]
                for i in range(p3n):
                    for half in range(2):
                        fc = 2 * i + half
                        fb = NFB * fc
                        for g0_ in range(2):
                            if fc > 0:
                                g.wait_ge(mm_o2, NFB * fc - 2 + 2 * g0_)
                            g.wait_ge(dw2, 16 * wd[0])
                            for m in range(2):
                                f = 2 * g0_ + m
                                off = ((f // 4) * WREG + BW20R + (f % 4) * 128 * 1024)
                                src = bass.AP(gatW, off, [[1024, 128], [1, 1024]])
                                g.dma_start(out=w2r[:, f % 4, :], in_=src).then_inc(dw2, 16)
                                wd[0] += 1
                        for f in range(NFB):
                            g.wait_ge(sc_g, fb + f + 1)
                            if fb + f >= 4:
                                g.wait_ge(ve_g, fb + f - 3)
                            g.tensor_tensor(out=difb[:, f % 4, :], in0=g1b[:, f % 2, :],
                                            in1=g0b[:, f % 2, :],
                                            op=ALU.subtract).then_inc(gp_c, 1)
                            if f % 2 == 1:
                                gw = (f + 3) // 2
                                if gw <= 15:
                                    g.wait_ge(mm_o2, fb + 2 * gw - 2)
                                    g.wait_ge(dw2, 16 * wd[0])
                                    for m in range(2):
                                        ff = 2 * gw + m
                                        off = ((ff // 4) * WREG + BW20R
                                               + (ff % 4) * 128 * 1024)
                                        src = bass.AP(gatW, off, [[1024, 128], [1, 1024]])
                                        g.dma_start(out=w2r[:, ff % 4, :],
                                                    in_=src).then_inc(dw2, 16)
                                        wd[0] += 1
                if p3n > 0:
                    g.wait_ge(dout[0], 16 * 2 * p3n)
                    g.wait_ge(dout[1], 16 * 2 * p3n)
                if dbg:
                    dtaps = [
                        (0, k2[:, 0:512]),
                        (512, xnf[:, 1, 0:512]),
                        (1024, vaug[:, 0:512]),
                        (1536, xnf[:, 0, 0:512]),
                        (2048, q2[:, 0:512]),
                        (2560, xbuf[:, 0, 0, 0:512]),
                        (3072, xbuf[:, 1, 0, 0:512]),
                        (3584, bass.AP(ybuf, 0, [[512, 128], [1, 512]])),
                    ]
                    for off_, ap_ in dtaps:
                        gdma(dbgD[:, off_:off_ + 512], ap_)
                    gdma(dbgD[:, 4096:4100], bass.AP(rb, 0, [[4, 128], [1, 4]]))
                    gdma(dbgD[:, 4100:4104], bass.AP(varb, 0, [[4, 128], [1, 4]]))
                    gdma(dbgD[:, 4104:4108], bass.AP(s1, 0, [[4, 128], [1, 4]]))
                    gdma(dbgD[:, 4108:4112], bass.AP(s2, 0, [[4, 128], [1, 4]]))
                    gdma(dbgD[:, 4112:4116], bass.AP(mbuf, 0, [[4, 128], [1, 4]]))
                    gdma(dbgD[:, 4116:4120], bass.AP(msq, 0, [[4, 128], [1, 4]]))
                    gdma(dbgD[:, 4120:4124], bass.AP(lnb, 0, [[4, 128], [1, 4]]))
                    gdma(dbgD[:, 4124:4128], bass.AP(mbb, 0, [[4, 128], [1, 4]]))
                    g.wait_ge(dg, 16 * gd[0])

            @block.vector
            def _(v):
                tkn = [0]

                def tick(instr):
                    instr.then_inc(ve_tk, 1)
                    tkn[0] += 1
                    v.wait_ge(ve_tk, tkn[0])

                v.wait_ge(dg, 16 * 3)
                v.tensor_copy(out=wq_bf[:, :], in_=xbuf[:, 0, 0, :]).then_inc(ve_c, 1)
                v.tensor_copy(out=wqq_bf[:, :], in_=xbuf[:, 0, 1, :]).then_inc(ve_c, 1)
                v.tensor_copy(out=wv_bf[:, :], in_=xbuf[:, 1, 0, 0:512]).then_inc(ve_c, 1)
                v.wait_ge(dg, 16 * 9)
                v.tensor_scalar(out=b1s[:, :], in0=b1s[:, :], scalar1=256.0, scalar2=None,
                                op0=ALU.mult).then_inc(ve_c, 1)
                v.tensor_scalar(out=b2b[:, :], in0=b2b[:, :], scalar1=256.0, scalar2=None,
                                op0=ALU.mult).then_inc(ve_c, 1)
                v.memset(vown[:, :], 1.0).then_inc(ve_c, 1)
                v.memset(c_q[:, :], 0.25).then_inc(ve_c, 1)
                v.memset(c_eps[:, :], 1e-5).then_inc(ve_c, 1)
                v.memset(yts[:, :], 0.0)
                v.memset(ytmp[:, :], 0.0).then_inc(ve_c, 1)
                at_flat = bass.AP(at, 0, [[2048, 128], [1, 2048]])
                of_flat = bass.AP(of32, 0, [[2048, 128], [1, 2048]])
                cast_at = {4: 0, 7: 1, 10: 2, 13: 3}
                cast_thr = (10, 13, 16, 18)
                casts_done = set()
                for tc in range(p1n):
                    sl = tc % 2
                    if p2 and tc in cast_at:
                        rnd = cast_at[tc]
                        v.wait_ge(dg, 16 * cast_thr[rnd])
                        v.tensor_copy(out=at_flat, in_=of_flat).then_inc(ve_w, 1)
                        casts_done.add(rnd)
                    if p1lvl < 2:
                        continue
                    v.wait_ge(dx[tc % 2], 16 * (tc // 2 + 1))
                    for t in range(2):
                        v.reduce_sum(out=s1[:, sl, t:t + 1], in_=xbuf[:, sl, t, :],
                                     axis=mybir.AxisListType.X)
                        v.scalar_tensor_tensor(out=obuf[:, :], in0=xbuf[:, sl, t, :],
                                               scalar=1.0, in1=xbuf[:, sl, t, :],
                                               op0=ALU.mult, op1=ALU.mult,
                                               accum_out=s2[:, sl, t:t + 1])
                    tick(v.tensor_scalar(out=mbuf[:, sl, :], in0=s1[:, sl, :],
                                         scalar1=1.0 / C, scalar2=None, op0=ALU.mult))
                    tick(v.tensor_tensor(out=msq[:, sl, :], in0=mbuf[:, sl, :],
                                         in1=mbuf[:, sl, :], op=ALU.mult))
                    v.scalar_tensor_tensor(out=varb[:, sl, :], in0=s2[:, sl, :],
                                           scalar=1.0 / C, in1=msq[:, sl, :],
                                           op0=ALU.mult, op1=ALU.subtract).then_inc(ve_st, 1)
                    if p1lvl < 3:
                        continue
                    v.wait_ge(sc_r, tc + 1)
                    v.scalar_tensor_tensor(out=mbb[:, sl, :], in0=mbuf[:, sl, :],
                                           scalar=-1.0, in1=rb[:, sl, :],
                                           op0=ALU.mult, op1=ALU.mult).then_inc(ve_mb, 1)
                    if p1lvl < 6:
                        continue
                    v.wait_ge(mm_tr, tc + 1)
                    if tc >= 2:
                        if p1lvl >= 67:
                            v.wait_ge(mm_qkv, tc - 1)
                        if p1lvl >= 66:
                            v.wait_ge(dxns[tc % 2], 16 * ((tc - 2) // 2 + 1))
                    for t in range(2):
                        ps = ps_s0 if t == 0 else ps_s1
                        for cb in range(NCB):
                            cp = v.tensor_copy(out=xnt[:, sl, cb, t * 128:(t + 1) * 128],
                                               in_=ps[:, cb * 128:(cb + 1) * 128])
                    cp.then_inc(ve_xn, 1)
                if not p2:
                    return
                for rnd in range(4):
                    if rnd in casts_done:
                        continue
                    v.wait_ge(dg, 16 * cast_thr[rnd])
                    v.tensor_copy(out=at_flat, in_=of_flat).then_inc(ve_w, 1)
                for i in range(p3n):
                    v.wait_ge(mm_avt, 128 * (i + 1))
                    v.tensor_copy(out=yts[0:65, :], in_=ps_a0[0:65, :]).then_inc(ve_av, 1)
                    v.wait_ge(mm_y, i + 1)
                    if i == 0:
                        v.wait_ge(dg, 16 * 7)
                        v.wait_ge(ve_c, 9)
                    for t in range(4):
                        tick(v.reciprocal(out=rcp[:, :],
                                          in_=ps_c[:, t * 128 + 64:t * 128 + 65]))
                        dv = v.scalar_tensor_tensor(out=ybuf[:, i % 2, t * 64:(t + 1) * 64],
                                                    in0=ps_c[:, t * 128:t * 128 + 64],
                                                    scalar=rcp[:, :], in1=bvb[:, :],
                                                    op0=ALU.mult, op1=ALU.add)
                    dv.then_inc(ve_y, 1)
                    for half in range(2):
                        fc = 2 * i + half
                        fb = NFB * fc
                        for f in range(NFB + 2):
                            if f < NFB:
                                v.wait_ge(mm_h, fb + f + 1)
                                off = (f % 2) * 512
                                v.tensor_scalar(out=y1b[:, :], in0=ps_s1[:, off:off + 256],
                                                scalar1=256.0, scalar2=b1s[:, f:f + 1],
                                                op0=ALU.mult, op1=ALU.add).then_inc(ve_y1, 1)
                                v.tensor_scalar(out=x0b[:, :], in0=y1b[:, :],
                                                scalar1=-1024.0, scalar2=None, op0=ALU.max)
                                v.tensor_scalar(out=tpb[:, f % 4, :], in0=x0b[:, :],
                                                scalar1=63, scalar2=None, op0=ALU.bitwise_and)
                                if fb + f >= 4:
                                    v.wait_ge(sc_g, fb + f - 3)
                                v.tensor_scalar(out=b0b[:, f % 4, :], in0=x0b[:, :],
                                                scalar1=-64, scalar2=None,
                                                op0=ALU.bitwise_and).then_inc(ve_pw, 1)
                            if f >= 2:
                                fz = f - 2
                                v.wait_ge(gp_c, fb + fz + 1)
                                v.scalar_tensor_tensor(out=z1b[:, :], in0=tpb[:, fz % 4, :],
                                                       scalar=4.0, in1=difb[:, fz % 4, :],
                                                       op0=ALU.mult, op1=ALU.mult)
                                v.scalar_tensor_tensor(out=g2b[:, :], in0=g0b[:, fz % 2, :],
                                                       scalar=256.0, in1=z1b[:, :],
                                                       op0=ALU.mult, op1=ALU.add)
                                if fb + fz >= 4:
                                    v.wait_ge(mm_o2, fb + fz - 3)
                                v.tensor_copy(out=gbfb[:, fz % 4, :],
                                              in_=g2b[:, :]).then_inc(ve_g, 1)
                        v.wait_ge(mm_o2, NFB * (fc + 1))
                        if fc == 0:
                            v.wait_ge(ve_c, 5)
                        v.wait_ge(dx[fc % 2], 16 * (NFC // 2 + fc // 2 + 1))
                        for t in range(2):
                            if fc >= 1:
                                v.wait_ge(dout[t], 16 * fc)
                            for hh in range(2):
                                q = 2 * t + hh
                                src = (ps_a0 if q == 0 else ps_a1 if q == 1 else ps_c)
                                srcap = (src[:, 0:512] if q != 3 else src[:, 512:1024])
                                v.tensor_tensor(out=z3b[:, :], in0=srcap,
                                                in1=b2b[:, hh * 512:(hh + 1) * 512], op=ALU.add)
                                v.tensor_scalar(out=x3b[:, :], in0=z3b[:, :], scalar1=1.0,
                                                scalar2=None, op0=ALU.mult)
                                v.scalar_tensor_tensor(
                                    out=obuf[:, hh * 512:(hh + 1) * 512],
                                    in0=x3b[:, :], scalar=1.0 / 256.0,
                                    in1=xbuf[:, fc % 2, t, hh * 512:(hh + 1) * 512],
                                    op0=ALU.mult, op1=ALU.add)
                            t4 = (2 * fc + t) % 4
                            rep = bass.AP(ybuf, (i % 2) * 256 + t4 * 64,
                                          [[512, 128], [0, 16], [1, 64]])
                            v.tensor_tensor(out=of32[:, t, :], in0=obuf[:, :],
                                            in1=rep, op=ALU.add).then_inc(ve_asm, 1)

            def qkv_evac(s, tcq):
                s.wait_ge(mm_qkv, tcq + 1)
                if tcq == 0:
                    s.wait_ge(dg, 16 * 6)
                    s.wait_ge(ve_c, 6)
                s.activation(out=qk[:, tcq * 256:(tcq + 1) * 256], in_=ps_a0[:, 0:256],
                             func=ACT.Identity, bias=bqk_s[:, :], scale=1.0)
                s.activation(out=q2[:, tcq * 256:(tcq + 1) * 256], in_=ps_a1[:, 0:256],
                             func=ACT.Identity, bias=bq2_s[:, :], scale=1.0)
                for t in range(2):
                    col = 65 * (2 * tcq + t)
                    ev = s.activation(out=vown[:, col:col + 64], in_=ps_c[:, t * 64:(t + 1) * 64],
                                      func=ACT.Identity, bias=0.0, scale=1.0)
                ev.then_inc(sc_qk, 1)

            @block.scalar
            def _(s):
                for tc in range(p1n):
                    sl = tc % 2
                    if p1lvl < 3:
                        continue
                    s.wait_ge(ve_st, tc + 1)
                    if tc == 0:
                        s.wait_ge(ve_c, 8)
                    s.activation(out=lnb[:, sl, :], in_=varb[:, sl, :], func=ACT.Ln,
                                 bias=c_eps[:, :], scale=1.0)
                    s.activation(out=rb[:, sl, :], in_=lnb[:, sl, :], func=ACT.Exp,
                                 bias=0.0, scale=-0.5).then_inc(sc_r, 1)
                    if p1lvl < 4:
                        continue
                    s.wait_ge(ve_mb, tc + 1)
                    if tc >= 1 and p1lvl >= 5:
                        s.wait_ge(mm_tr, tc)
                    for t in range(2):
                        nr = s.activation(out=xnf[:, t, :], in_=xbuf[:, sl, t, :],
                                          func=ACT.Identity, bias=mbb[:, sl, t:t + 1],
                                          scale=rb[:, sl, t:t + 1])
                    nr.then_inc(sc_n, 1)
                    if tc >= 1 and p1lvl >= 67:
                        qkv_evac(s, tc - 1)
                if p1n >= 1 and p1lvl >= 67:
                    qkv_evac(s, p1n - 1)
                for i in range(p3n):
                    for pr in range(128):
                        gp = 128 * i + pr
                        s.wait_ge(mm_s, gp + 1)
                        if gp >= 2:
                            s.wait_ge(mm_avt, gp - 1)
                        if gp < 2:
                            s.wait_ge(dg, 16 * DG_W_DONE)
                        reg = ps_s0 if pr % 2 == 0 else ps_s1
                        s.activation(out=at[:, pr % 2, :], in_=reg[:, :], func=ACT.Exp,
                                     bias=0.0, scale=1.0 / 32.0).then_inc(exp_c, 1)
                    for half in range(2):
                        fc = 2 * i + half
                        fb = NFB * fc
                        for f in range(NFB):
                            s.wait_ge(ve_pw, fb + f + 1)
                            if fb + f == 0:
                                s.wait_ge(ve_c, 7)
                            if fb + f >= 2:
                                s.wait_ge(gp_c, fb + f - 1)
                                s.wait_ge(ve_g, fb + f - 1)
                            s.activation(out=g0b[:, f % 2, :], in_=b0b[:, f % 4, :],
                                         func=ACT.Gelu, bias=0.0, scale=1.0 / 256.0)
                            s.activation(out=g1b[:, f % 2, :], in_=b0b[:, f % 4, :],
                                         func=ACT.Gelu, bias=c_q[:, :],
                                         scale=1.0 / 256.0).then_inc(sc_g, 1)

            @block.tensor
            def _(t):
                for tc in range(p1n + 1):
                    if p1lvl < 5:
                        continue
                    if tc < p1n:
                        t.wait_ge(sc_n, tc + 1)
                        if tc == 0:
                            t.wait_ge(dg, 16 * 4)
                        if tc >= 1:
                            t.wait_ge(ve_xn, tc)
                        t.drain()
                        for ti in range(2):
                            ps = ps_s0 if ti == 0 else ps_s1
                            for cb in range(NCB):
                                tr = t.transpose(ps[:, cb * 128:(cb + 1) * 128],
                                                 xnf[:, ti, cb * 128:(cb + 1) * 128], idf[:, :])
                        tr.then_inc(mm_tr, 1)
                    if tc >= 1 and p1lvl >= 67:
                        tcq = tc - 1
                        t.wait_ge(ve_xn, tcq + 1)
                        if tcq == 0:
                            t.wait_ge(ve_c, 3)
                        if tcq >= 1:
                            t.wait_ge(sc_qk, tcq)
                        sl = tcq % 2
                        for cb in range(NCB):
                            t.matmul(ps_a0[:, 0:256], wq_bf[:, cb * 128:(cb + 1) * 128],
                                     xnt[:, sl, cb, :], start=(cb == 0), stop=(cb == NCB - 1))
                        for cb in range(NCB):
                            t.matmul(ps_a1[:, 0:256], wqq_bf[:, cb * 128:(cb + 1) * 128],
                                     xnt[:, sl, cb, :], start=(cb == 0), stop=(cb == NCB - 1))
                        for ti in range(2):
                            for cb in range(NCB):
                                vm = t.matmul(ps_c[:, ti * 64:(ti + 1) * 64],
                                              xnt[:, sl, cb, ti * 128:(ti + 1) * 128],
                                              wv_bf[:, cb * 64:(cb + 1) * 64],
                                              start=(cb == 0), stop=(cb == NCB - 1))
                        vm.then_inc(mm_qkv, 1)
                for i in range(p3n if p2 else 0):
                    t.wait_ge(dg, 16 * DG_READY)
                    if i == 0:
                        t.wait_ge(sc_qk, NFC)
                    if i >= 1:
                        t.wait_ge(ve_asm, 4 * i)
                    for pr in range(129):
                        if pr < 128:
                            gp = 128 * i + pr
                            if gp >= 2:
                                t.wait_ge(exp_c, gp - 1)
                            if pr == 1 and i >= 1:
                                t.wait_ge(ve_y1, NFB * 2 * i)
                            reg = ps_s0 if pr % 2 == 0 else ps_s1
                            t.matmul(reg[:, 0:512], k2[0:64, pr * 128:(pr + 1) * 128],
                                     q2[0:64, i * QC:(i + 1) * QC], start=True, stop=True,
                                     tile_position=(0, 0))
                            t.matmul(reg[:, 512:1024], k2[64:128, pr * 128:(pr + 1) * 128],
                                     q2[64:128, i * QC:(i + 1) * QC], start=True, stop=True,
                                     tile_position=(64, 0)).then_inc(mm_s, 1)
                        if pr >= 1:
                            pa = pr - 1
                            t.wait_ge(exp_c, 128 * i + pa + 1)
                            first = pa == 0
                            last = pa == 127
                            for ko in range(2):
                                e = 2 * pa + ko
                                acol = ko * 512
                                av = t.matmul(ps_a0[0:65, :], vaug[:, 65 * e:65 * e + 65],
                                              at[:, pa % 2, acol:acol + 512],
                                              start=(first and ko == 0),
                                              stop=(last and ko == 1))
                            av.then_inc(mm_avt, 1)
                    t.drain()
                    t.wait_ge(ve_av, i + 1)
                    for ti in range(4):
                        yt = t.transpose(ps_c[:, ti * 128:ti * 128 + 66],
                                         yts[0:66, ti * 128:(ti + 1) * 128], idf[0:66, 0:66])
                    yt.then_inc(mm_y, 1)
                    t.drain()
                    for half in range(2):
                        fc = 2 * i + half
                        fb = NFB * fc
                        for f in range(NFB + 2):
                            if f < NFB:
                                t.wait_ge(dw1[f % 4], 16 * ((fb + f) // 4 + 1))
                                t.wait_ge(dxnl[fc % 2], 16 * (fc // 2 + 1))
                                if fb + f >= 2:
                                    t.wait_ge(ve_y1, fb + f - 1)
                                if f < 2:
                                    t.wait_ge(exp_c, 128 * (i + 1))
                                off = (f % 2) * 512
                                for cb in range(NCB):
                                    hm = t.matmul(ps_s1[:, off:off + 256],
                                                  w1r[:, f % 4, cb, :], xnt[:, fc % 2, cb, :],
                                                  start=(cb == 0), stop=(cb == NCB - 1))
                                hm.then_inc(mm_h, 1)
                            if f >= 2:
                                fz = f - 2
                                t.wait_ge(ve_g, fb + fz + 1)
                                t.wait_ge(dw2, 16 * (fb + 2 * (fz // 2 + 1)))
                                if fz < 2:
                                    if half == 0:
                                        t.wait_ge(ve_y, i + 1)
                                        t.wait_ge(ve_av, i + 1)
                                    else:
                                        t.wait_ge(ve_asm, 4 * i + 2)
                                for ti in range(2):
                                    for hh in range(2):
                                        q = 2 * ti + hh
                                        dst = (ps_a0 if q == 0 else ps_a1 if q == 1 else ps_c)
                                        dstap = (dst[:, 0:512] if q != 3 else dst[:, 512:1024])
                                        om = t.matmul(dstap,
                                                      gbfb[:, fz % 4, ti * 128:(ti + 1) * 128],
                                                      w2r[:, fz % 4, hh * 512:(hh + 1) * 512],
                                                      start=(fz == 0), stop=(fz == NFB - 1))
                                om.then_inc(mm_o2, 1)
    return nc


def kernel(L0_attn_embeddings, L0_attn_Wq, L0_attn_bq, L0_attn_Wk, L0_attn_bk,
           L0_attn_Wv, L0_attn_bv, L0_ffn_w1, L0_ffn_b1, L0_ffn_w2, L0_ffn_b2):
    x = np.ascontiguousarray(np.asarray(L0_attn_embeddings, dtype=np.float32)
                             .reshape(NCORES * TOK, C))
    Wq = np.asarray(L0_attn_Wq, dtype=np.float32)
    Wk = np.asarray(L0_attn_Wk, dtype=np.float32)
    Wv = np.asarray(L0_attn_Wv, dtype=np.float32)
    bq = np.asarray(L0_attn_bq, dtype=np.float32)
    bk = np.asarray(L0_attn_bk, dtype=np.float32)
    bv = np.asarray(L0_attn_bv, dtype=np.float32)
    w1 = np.asarray(L0_ffn_w1, dtype=np.float32)
    b1 = np.asarray(L0_ffn_b1, dtype=np.float32)
    w2 = np.asarray(L0_ffn_w2, dtype=np.float32)
    b2 = np.asarray(L0_ffn_b2, dtype=np.float32)

    wqkT_ = np.ascontiguousarray(np.concatenate([Wq.T, Wk.T], axis=1))
    wqqT_ = np.ascontiguousarray(np.concatenate([Wq.T, Wq.T], axis=1))
    wvT_ = np.ascontiguousarray(Wv.T)
    bqk = np.ascontiguousarray(np.concatenate([bq, bk]))
    bq2 = np.ascontiguousarray(np.concatenate([bq, bq]))
    idf = np.eye(128, dtype=np.float32)

    nc = build_nc()
    in_maps = []
    for r in range(NCORES):
        in_maps.append({
            "x": np.ascontiguousarray(x[r * TOK:(r + 1) * TOK]),
            "wqkT": wqkT_, "wqqT": wqqT_, "wvT": wvT_,
            "bqk": bqk, "bq2": bq2, "bv": bv,
            "w1s": np.ascontiguousarray(w1[:, r * 512:(r + 1) * 512]),
            "w2s": np.ascontiguousarray(w2[r * 512:(r + 1) * 512, :]),
            "b1r": np.ascontiguousarray(b1.reshape(32, 128).T), "b2": b2, "idf": idf,
        })
    res = run_bass_kernel_spmd(nc, in_maps, core_ids=list(range(NCORES)))
    out = np.concatenate([np.asarray(r["out"]) for r in res.results], axis=0)
    return out.reshape(NCORES, TOK, C).astype(np.float32)


# version 13 (final)
# speedup vs baseline: 1.0159x; 1.0019x over previous
import numpy as np
from contextlib import ExitStack

import concourse.bass as bass
import concourse.mybir as mybir
from concourse.bass_utils import run_bass_kernel_spmd

F32 = mybir.dt.float32
BF16 = mybir.dt.bfloat16
I32 = mybir.dt.int32
ACT = mybir.ActivationFunctionType
ALU = mybir.AluOpType

NCORES = 8
TOK = 4096
C = 1024
FF = 4096
QC = 512
NQC = TOK // QC
FC = 256
NFC = TOK // FC
NCB = C // 128
NFB = FF // 128

K_ELEMS = 64 * TOK
V_ELEMS = 128 * 2080
W1_ELEMS = 4 * 1024 * 128
W2_ELEMS = 512 * 1024
BK0 = 0
BV0 = K_ELEMS
BW10 = BV0 + V_ELEMS
BW20 = BW10 + W1_ELEMS
BN_ELEMS = BW20 + W2_ELEMS
KVREG = BW10
WREG = BN_ELEMS - BW10
BW20R = BW20 - BW10

DG_W_DONE = 19
DG_BOUNCED = 21
DG_READY = 45


def build_nc(p2=True, p3n=NQC, p1n=NFC, p1lvl=67, dbg=False):
    nc = bass.Bass("TRN2", target_bir_lowering=False, detect_race_conditions=False)

    xD = nc.dram_tensor("x", [TOK, C], F32, kind="ExternalInput")
    wqkT = nc.dram_tensor("wqkT", [C, 128], F32, kind="ExternalInput")
    wqqT = nc.dram_tensor("wqqT", [C, 128], F32, kind="ExternalInput")
    wvT = nc.dram_tensor("wvT", [C, 64], F32, kind="ExternalInput")
    bqkD = nc.dram_tensor("bqk", [128], F32, kind="ExternalInput")
    bq2D = nc.dram_tensor("bq2", [128], F32, kind="ExternalInput")
    bvD = nc.dram_tensor("bv", [64], F32, kind="ExternalInput")
    w1sD = nc.dram_tensor("w1s", [C, 512], F32, kind="ExternalInput")
    w2sD = nc.dram_tensor("w2s", [512, C], F32, kind="ExternalInput")
    b1D = nc.dram_tensor("b1r", [128, 32], F32, kind="ExternalInput")
    b2D = nc.dram_tensor("b2", [C], F32, kind="ExternalInput")
    idD = nc.dram_tensor("idf", [128, 128], F32, kind="ExternalInput")
    outD = nc.dram_tensor("out", [TOK, C], F32, kind="ExternalOutput")
    dbgD = nc.dram_tensor("dbg", [128, 4224], F32, kind="ExternalOutput") if dbg else None

    xntD = nc.dram_tensor("xnt_hbm", [128, TOK * NCB], BF16)
    bncD = nc.dram_tensor("bounce", [BN_ELEMS], BF16)
    gatW = nc.dram_tensor("gatw", [NCORES * WREG], BF16, addr_space="Shared")
    gatKV = nc.dram_tensor("gatkv", [NCORES * KVREG], BF16, addr_space="Shared")

    es = ExitStack()
    S = lambda name, shp, dt: es.enter_context(nc.sbuf_tensor(name, shp, dt))
    P = lambda name, shp, dt: es.enter_context(nc.psum_tensor(name, shp, dt))
    SEM = lambda name: es.enter_context(nc.semaphore(name))

    k2 = S("k2", [128, 16384], BF16)
    vaug = S("vaug", [128, 16640], BF16)
    q2 = S("q2", [128, TOK], BF16)
    qk = S("qk", [128, TOK], BF16)
    xnt = S("xnt", [128, 2, NCB, 256], BF16)
    xbuf = S("xbuf", [128, 2, 2, 1024], F32)
    xnf = S("xnf", [128, 2, 1024], F32)
    wq_bf = S("wq_bf", [128, 1024], BF16)
    wqq_bf = S("wqq_bf", [128, 1024], BF16)
    wv_bf = S("wv_bf", [128, 512], BF16)
    idf = S("idf_s", [128, 128], F32)
    bqk_s = S("bqk_s", [128, 1], F32)
    bq2_s = S("bq2_s", [128, 1], F32)
    c_eps = S("c_eps", [128, 1], F32)
    c_q = S("c_q", [128, 1], F32)
    b1s = S("b1s", [128, NFB], F32)
    b2b = S("b2b", [128, 1024], F32)
    bvb = S("bvb", [128, 64], F32)
    w1r = S("w1r", [128, 4, NCB, 128], BF16)
    w2r = S("w2r", [128, 4, 1024], BF16)
    at = S("at", [128, 2, 1024], BF16)
    yts = S("yts", [128, 512], F32)
    ytmp = S("ytmp", [128, 512], F32)
    rcp = S("rcp", [128, 1], F32)
    ybuf = S("ybuf", [128, 2, 256], F32)
    y1b = S("y1b", [128, 256], F32)
    x0b = S("x0b", [128, 256], I32)
    tpb = S("tpb", [128, 4, 256], I32)
    b0b = S("b0b", [128, 4, 256], I32)
    g0b = S("g0b", [128, 2, 256], F32)
    g1b = S("g1b", [128, 2, 256], F32)
    difb = S("difb", [128, 4, 256], F32)
    z1b = S("z1b", [128, 256], F32)
    g2b = S("g2b", [128, 256], I32)
    gbfb = S("gbfb", [128, 4, 256], BF16)
    z3b = S("z3b", [128, 512], F32)
    x3b = S("x3b", [128, 512], I32)
    of32 = S("of32", [128, 2, 2, 1024], F32)
    obuf = S("obuf", [128, 1024], F32)
    s1 = S("s1", [128, 2, 2], F32)
    s2 = S("s2", [128, 2, 2], F32)
    mbuf = S("mbuf", [128, 2, 2], F32)
    varb = S("varb", [128, 2, 2], F32)
    rb = S("rb", [128, 2, 2], F32)
    mbb = S("mbb", [128, 2, 2], F32)
    msq = S("msq", [128, 2, 2], F32)
    lnb = S("lnb", [128, 2, 2], F32)
    vown = S("vown", [128, 2080], BF16)

    ps_s0 = P("ps_s0", [128, 1024], F32)
    ps_s1 = P("ps_s1", [128, 1024], F32)
    ps_a0 = P("ps_a0", [128, 512], F32)
    ps_a1 = P("ps_a1", [128, 512], F32)
    ps_c = P("ps_c", [128, 1024], F32)

    dx = [SEM("dx0"), SEM("dx1")]
    dxns = [SEM("dxns0"), SEM("dxns1")]
    dxnl = [SEM("dxnl0"), SEM("dxnl1")]
    dw1 = [SEM("dw10"), SEM("dw11"), SEM("dw12"), SEM("dw13")]
    dw2 = SEM("dw2")
    dg = SEM("dg")
    dout = [SEM("dout0"), SEM("dout1")]
    cc = SEM("cc")
    ve_c = SEM("ve_c")
    ve_w = SEM("ve_w")
    ve_st = SEM("ve_st")
    sc_r = SEM("sc_r")
    ve_mb = SEM("ve_mb")
    sc_n = SEM("sc_n")
    mm_tr = SEM("mm_tr")
    ve_xn = SEM("ve_xn")
    mm_qkv = SEM("mm_qkv")
    sc_qk = SEM("sc_qk")
    mm_s = SEM("mm_s")
    exp_c = SEM("exp_c")
    mm_avt = SEM("mm_avt")
    ve_av = SEM("ve_av")
    mm_y = SEM("mm_y")
    ve_y = SEM("ve_y")
    mm_h = SEM("mm_h")
    ve_y1 = SEM("ve_y1")
    ve_pw = SEM("ve_pw")
    sc_g = SEM("sc_g")
    gp_c = SEM("gp_c")
    ve_g = SEM("ve_g")
    mm_o2 = SEM("mm_o2")
    ve_asm = SEM("ve_asm")
    ve_tk = SEM("ve_tk")

    with es:
        with nc.Block() as block:

            @block.sync
            def _(sync):
                for tc in range(p1n):
                    if tc < 2:
                        sync.wait_ge(ve_c, 3)
                    if tc >= 2:
                        if p1lvl >= 4:
                            sync.wait_ge(sc_n, tc - 1)
                        if p1lvl >= 2:
                            sync.wait_ge(ve_st, tc - 1)
                    src = bass.AP(xD, tc * FC * C, [[C, 128], [128 * C, 2], [1, C]])
                    sync.dma_start(out=xbuf[:, tc % 2, :, :], in_=src).then_inc(dx[tc % 2], 16)
                    if p1lvl >= 66:
                        sync.wait_ge(ve_xn, tc + 1)
                        dst = bass.AP(xntD, tc * 256, [[TOK * NCB, 128], [TOK, NCB], [1, 256]])
                        sync.dma_start(out=dst, in_=xnt[:, tc % 2, :, :]).then_inc(dxns[tc % 2], 16)
                for fc in range(2 * p3n):
                    if fc == 0:
                        sync.wait_ge(dxns[0], 16 * (NFC // 2))
                        sync.wait_ge(dxns[1], 16 * (NFC // 2))
                        sync.wait_ge(cc, 1)
                    if fc >= 2:
                        sync.wait_ge(mm_h, NFB * (fc - 1))
                    src = bass.AP(xntD, fc * 256, [[TOK * NCB, 128], [TOK, NCB], [1, 256]])
                    sync.dma_start(out=xnt[:, fc % 2, :, :], in_=src).then_inc(dxnl[fc % 2], 16)
                    if fc >= 2:
                        sync.wait_ge(ve_asm, 2 * (fc - 2) + 2)
                    src = bass.AP(xD, fc * FC * C, [[C, 128], [128 * C, 2], [1, C]])
                    sync.dma_start(out=xbuf[:, fc % 2, :, :], in_=src).then_inc(dx[fc % 2], 16)
                    for g in range(16):
                        if fc > 0 or g >= 2:
                            if g == 0:
                                sync.wait_ge(mm_h, NFB * (fc - 1) + 30)
                            elif g == 1:
                                sync.wait_ge(mm_h, NFB * fc)
                            else:
                                sync.wait_ge(mm_h, NFB * fc + 2 * g - 2)
                        for m in range(2):
                            f = 2 * g + m
                            rk = f // 4
                            lf = f % 4
                            off = rk * WREG + lf * (1024 * 128)
                            src = bass.AP(gatW, off, [[128, 128], [128 * 128, NCB], [1, 128]])
                            sync.dma_start(out=w1r[:, f % 4, :, :], in_=src).then_inc(dw1[f % 4], 16)
                    if fc >= 1:
                        for t in range(2):
                            sync.wait_ge(ve_asm, 2 * (fc - 1) + t + 1)
                            dst = bass.AP(outD, ((fc - 1) * FC + t * 128) * C, [[C, 128], [1, C]])
                            sync.dma_start(out=dst,
                                           in_=of32[:, (fc - 1) % 2, t, :]).then_inc(dout[t], 16)
                if p3n > 0:
                    for t in range(2):
                        sync.wait_ge(ve_asm, 2 * (2 * p3n - 1) + t + 1)
                        dst = bass.AP(outD, ((2 * p3n - 1) * FC + t * 128) * C,
                                      [[C, 128], [1, C]])
                        sync.dma_start(out=dst,
                                       in_=of32[:, (2 * p3n - 1) % 2, t, :]).then_inc(dout[t], 16)

            @block.gpsimd
            def _(g):
                gd = [0]

                def gdma(dst, src, drain=False):
                    g.dma_start(out=dst, in_=src).then_inc(dg, 16)
                    gd[0] += 1
                    if drain:
                        g.wait_ge(dg, 16 * gd[0])

                gdma(xbuf[:, 0, 0, :], bass.AP(wqkT, 0, [[128, 128], [128 * 128, NCB], [1, 128]]), drain=True)
                gdma(xbuf[:, 0, 1, :], bass.AP(wqqT, 0, [[128, 128], [128 * 128, NCB], [1, 128]]), drain=True)
                gdma(xbuf[:, 1, 0, 0:512], bass.AP(wvT, 0, [[64, 128], [64 * 128, NCB], [1, 64]]), drain=True)
                gdma(idf[:, :], idD[:, :], drain=True)
                gdma(bqk_s[:, :], bass.AP(bqkD, 0, [[1, 128], [1, 1]]), drain=True)
                gdma(bq2_s[:, :], bass.AP(bq2D, 0, [[1, 128], [1, 1]]), drain=True)
                gdma(bvb[:, :], bass.AP(bvD, 0, [[0, 128], [1, 64]]), drain=True)
                gdma(b1s[:, :], b1D[:, :], drain=True)
                gdma(b2b[:, :], bass.AP(b2D, 0, [[0, 128], [1, 1024]]), drain=True)
                of_flat = bass.AP(of32, 0, [[4096, 128], [1, 2048]])
                if not p2:
                    if p1lvl >= 66:
                        g.wait_ge(dxns[0], 16 * ((p1n + 1) // 2))
                        g.wait_ge(dxns[1], 16 * (p1n // 2))
                    elif p1lvl >= 6:
                        g.wait_ge(ve_xn, p1n)
                    if p1lvl >= 67:
                        g.wait_ge(sc_qk, p1n)
                    if p1lvl == 5:
                        g.wait_ge(mm_tr, p1n)
                    if p1lvl == 4:
                        g.wait_ge(sc_n, p1n)
                    if p1lvl == 3:
                        g.wait_ge(ve_mb, p1n)
                    if p1lvl == 2:
                        g.wait_ge(ve_st, p1n)
                    if p1lvl == 1:
                        g.wait_ge(dx[0], 16 * ((p1n + 1) // 2))
                        g.wait_ge(dx[1], 16 * (p1n // 2))
                    return
                for rnd in range(2):
                    gdma(of_flat, bass.AP(w1sD, rnd * 256,
                                          [[512, 128], [512 * 128, NCB], [1, 256]]))
                    g.wait_ge(ve_w, rnd + 1)
                    for m in range(2):
                        f_loc = 2 * rnd + m
                        dst = bass.AP(bncD, BW10 + f_loc * (1024 * 128),
                                      [[128, 128], [128 * 128, NCB], [1, 128]])
                        src = bass.AP(at, m * 128, [[2048, 128], [256, NCB], [1, 128]])
                        gdma(dst, src)
                for rnd in range(2):
                    gdma(of_flat, bass.AP(w2sD, rnd * 256 * 1024,
                                          [[1024, 128], [128 * 1024, 2], [1, 1024]]))
                    g.wait_ge(ve_w, 3 + rnd)
                    dst = bass.AP(bncD, BW20 + rnd * 256 * 1024,
                                  [[1024, 128], [128 * 1024, 2], [1, 1024]])
                    gdma(dst, bass.AP(at, 0, [[2048, 128], [1024, 2], [1, 1024]]))
                g.wait_ge(dg, 16 * gd[0])
                g.collective_compute(
                    "AllGather", ALU.bypass,
                    replica_groups=[list(range(NCORES))],
                    ins=[bass.AP(bncD, BW10, [[1, WREG]]).opt()],
                    outs=[gatW.ap().opt()],
                ).then_inc(cc, 1)
                g.wait_ge(sc_qk, p1n)
                gdma(bass.AP(bncD, BK0, [[TOK, 64], [1, TOK]]), qk[64:128, :])
                gdma(bass.AP(bncD, BV0, [[2080, 128], [1, 2080]]), vown[:, :])
                assert gd[0] == DG_BOUNCED, gd[0]
                g.wait_ge(dg, 16 * DG_BOUNCED)
                g.collective_compute(
                    "AllGather", ALU.bypass,
                    replica_groups=[list(range(NCORES))],
                    ins=[bass.AP(bncD, BK0, [[1, KVREG]]).opt()],
                    outs=[gatKV.ap().opt()],
                ).then_inc(cc, 1)
                g.wait_ge(cc, 2)
                for r in range(NCORES):
                    src = bass.AP(gatKV, r * KVREG + BK0, [[TOK, 64], [256, 16], [1, 128]])
                    gdma(k2[0:64, r * 2048:(r + 1) * 2048], src)
                for r in range(NCORES):
                    src = bass.AP(gatKV, r * KVREG + BK0 + 128,
                                  [[TOK, 64], [256, 16], [1, 128]])
                    gdma(k2[64:128, r * 2048:(r + 1) * 2048], src)
                for r in range(NCORES):
                    src = bass.AP(gatKV, r * KVREG + BV0, [[2080, 128], [1, 2080]])
                    gdma(vaug[:, r * 2080:(r + 1) * 2080], src)
                assert gd[0] == DG_READY, gd[0]
                wd = [0]
                for i in range(p3n):
                    for half in range(2):
                        fc = 2 * i + half
                        fb = NFB * fc
                        for g0_ in range(2):
                            if fc > 0:
                                g.wait_ge(mm_o2, NFB * fc - 2 + 2 * g0_)
                            g.wait_ge(dw2, 16 * wd[0])
                            for m in range(2):
                                f = 2 * g0_ + m
                                off = ((f // 4) * WREG + BW20R + (f % 4) * 128 * 1024)
                                src = bass.AP(gatW, off, [[1024, 128], [1, 1024]])
                                g.dma_start(out=w2r[:, f % 4, :], in_=src).then_inc(dw2, 16)
                                wd[0] += 1
                        for f in range(NFB):
                            g.wait_ge(sc_g, fb + f + 1)
                            if fb + f >= 4:
                                g.wait_ge(ve_g, fb + f - 3)
                            g.tensor_tensor(out=difb[:, f % 4, :], in0=g1b[:, f % 2, :],
                                            in1=g0b[:, f % 2, :],
                                            op=ALU.subtract).then_inc(gp_c, 1)
                            if f % 2 == 1:
                                gw = (f + 3) // 2
                                if gw <= 15:
                                    g.wait_ge(mm_o2, fb + 2 * gw - 2)
                                    g.wait_ge(dw2, 16 * wd[0])
                                    for m in range(2):
                                        ff = 2 * gw + m
                                        off = ((ff // 4) * WREG + BW20R
                                               + (ff % 4) * 128 * 1024)
                                        src = bass.AP(gatW, off, [[1024, 128], [1, 1024]])
                                        g.dma_start(out=w2r[:, ff % 4, :],
                                                    in_=src).then_inc(dw2, 16)
                                        wd[0] += 1
                if p3n > 0:
                    g.wait_ge(dout[0], 16 * 2 * p3n)
                    g.wait_ge(dout[1], 16 * 2 * p3n)
                if dbg:
                    dtaps = [
                        (0, k2[:, 0:512]),
                        (512, xnf[:, 1, 0:512]),
                        (1024, vaug[:, 0:512]),
                        (1536, xnf[:, 0, 0:512]),
                        (2048, q2[:, 0:512]),
                        (2560, xbuf[:, 0, 0, 0:512]),
                        (3072, xbuf[:, 1, 0, 0:512]),
                        (3584, bass.AP(ybuf, 0, [[512, 128], [1, 512]])),
                    ]
                    for off_, ap_ in dtaps:
                        gdma(dbgD[:, off_:off_ + 512], ap_)
                    gdma(dbgD[:, 4096:4100], bass.AP(rb, 0, [[4, 128], [1, 4]]))
                    gdma(dbgD[:, 4100:4104], bass.AP(varb, 0, [[4, 128], [1, 4]]))
                    gdma(dbgD[:, 4104:4108], bass.AP(s1, 0, [[4, 128], [1, 4]]))
                    gdma(dbgD[:, 4108:4112], bass.AP(s2, 0, [[4, 128], [1, 4]]))
                    gdma(dbgD[:, 4112:4116], bass.AP(mbuf, 0, [[4, 128], [1, 4]]))
                    gdma(dbgD[:, 4116:4120], bass.AP(msq, 0, [[4, 128], [1, 4]]))
                    gdma(dbgD[:, 4120:4124], bass.AP(lnb, 0, [[4, 128], [1, 4]]))
                    gdma(dbgD[:, 4124:4128], bass.AP(mbb, 0, [[4, 128], [1, 4]]))
                    g.wait_ge(dg, 16 * gd[0])

            @block.vector
            def _(v):
                tkn = [0]

                def tick(instr):
                    instr.then_inc(ve_tk, 1)
                    tkn[0] += 1
                    v.wait_ge(ve_tk, tkn[0])

                v.wait_ge(dg, 16 * 3)
                v.tensor_copy(out=wq_bf[:, :], in_=xbuf[:, 0, 0, :]).then_inc(ve_c, 1)
                v.tensor_copy(out=wqq_bf[:, :], in_=xbuf[:, 0, 1, :]).then_inc(ve_c, 1)
                v.tensor_copy(out=wv_bf[:, :], in_=xbuf[:, 1, 0, 0:512]).then_inc(ve_c, 1)
                v.wait_ge(dg, 16 * 9)
                v.tensor_scalar(out=b1s[:, :], in0=b1s[:, :], scalar1=256.0, scalar2=None,
                                op0=ALU.mult).then_inc(ve_c, 1)
                v.tensor_scalar(out=b2b[:, :], in0=b2b[:, :], scalar1=256.0, scalar2=None,
                                op0=ALU.mult).then_inc(ve_c, 1)
                v.memset(vown[:, :], 1.0).then_inc(ve_c, 1)
                v.memset(c_q[:, :], 0.25).then_inc(ve_c, 1)
                v.memset(c_eps[:, :], 1e-5).then_inc(ve_c, 1)
                v.memset(yts[:, :], 0.0)
                v.memset(ytmp[:, :], 0.0).then_inc(ve_c, 1)
                at_flat = bass.AP(at, 0, [[2048, 128], [1, 2048]])
                of_flat = bass.AP(of32, 0, [[4096, 128], [1, 2048]])
                cast_at = {4: 0, 7: 1, 10: 2, 13: 3}
                cast_thr = (10, 13, 16, 18)
                casts_done = set()
                for tc in range(p1n):
                    sl = tc % 2
                    if p2 and tc in cast_at:
                        rnd = cast_at[tc]
                        v.wait_ge(dg, 16 * cast_thr[rnd])
                        v.tensor_copy(out=at_flat, in_=of_flat).then_inc(ve_w, 1)
                        casts_done.add(rnd)
                    if p1lvl < 2:
                        continue
                    v.wait_ge(dx[tc % 2], 16 * (tc // 2 + 1))
                    for t in range(2):
                        v.reduce_sum(out=s1[:, sl, t:t + 1], in_=xbuf[:, sl, t, :],
                                     axis=mybir.AxisListType.X)
                        v.scalar_tensor_tensor(out=obuf[:, :], in0=xbuf[:, sl, t, :],
                                               scalar=1.0, in1=xbuf[:, sl, t, :],
                                               op0=ALU.mult, op1=ALU.mult,
                                               accum_out=s2[:, sl, t:t + 1])
                    tick(v.tensor_scalar(out=mbuf[:, sl, :], in0=s1[:, sl, :],
                                         scalar1=1.0 / C, scalar2=None, op0=ALU.mult))
                    tick(v.tensor_tensor(out=msq[:, sl, :], in0=mbuf[:, sl, :],
                                         in1=mbuf[:, sl, :], op=ALU.mult))
                    v.scalar_tensor_tensor(out=varb[:, sl, :], in0=s2[:, sl, :],
                                           scalar=1.0 / C, in1=msq[:, sl, :],
                                           op0=ALU.mult, op1=ALU.subtract).then_inc(ve_st, 1)
                    if p1lvl < 3:
                        continue
                    v.wait_ge(sc_r, tc + 1)
                    v.scalar_tensor_tensor(out=mbb[:, sl, :], in0=mbuf[:, sl, :],
                                           scalar=-1.0, in1=rb[:, sl, :],
                                           op0=ALU.mult, op1=ALU.mult).then_inc(ve_mb, 1)
                    if p1lvl < 6:
                        continue
                    v.wait_ge(mm_tr, tc + 1)
                    if tc >= 2:
                        if p1lvl >= 67:
                            v.wait_ge(mm_qkv, tc - 1)
                        if p1lvl >= 66:
                            v.wait_ge(dxns[tc % 2], 16 * ((tc - 2) // 2 + 1))
                    for t in range(2):
                        ps = ps_s0 if t == 0 else ps_s1
                        for cb in range(NCB):
                            cp = v.tensor_copy(out=xnt[:, sl, cb, t * 128:(t + 1) * 128],
                                               in_=ps[:, cb * 128:(cb + 1) * 128])
                    cp.then_inc(ve_xn, 1)
                if not p2:
                    return
                for rnd in range(4):
                    if rnd in casts_done:
                        continue
                    v.wait_ge(dg, 16 * cast_thr[rnd])
                    v.tensor_copy(out=at_flat, in_=of_flat).then_inc(ve_w, 1)
                for i in range(p3n):
                    v.wait_ge(mm_avt, 128 * (i + 1))
                    v.tensor_copy(out=yts[0:65, :], in_=ps_a0[0:65, :]).then_inc(ve_av, 1)
                    v.wait_ge(mm_y, i + 1)
                    if i == 0:
                        v.wait_ge(dg, 16 * 7)
                        v.wait_ge(ve_c, 9)
                    for t in range(4):
                        tick(v.reciprocal(out=rcp[:, :],
                                          in_=ps_c[:, t * 128 + 64:t * 128 + 65]))
                        dv = v.scalar_tensor_tensor(out=ybuf[:, i % 2, t * 64:(t + 1) * 64],
                                                    in0=ps_c[:, t * 128:t * 128 + 64],
                                                    scalar=rcp[:, :], in1=bvb[:, :],
                                                    op0=ALU.mult, op1=ALU.add)
                    dv.then_inc(ve_y, 1)
                    for half in range(2):
                        fc = 2 * i + half
                        fb = NFB * fc
                        for f in range(NFB + 2):
                            if f < NFB:
                                v.wait_ge(mm_h, fb + f + 1)
                                off = (f % 2) * 512
                                v.tensor_scalar(out=y1b[:, :], in0=ps_s1[:, off:off + 256],
                                                scalar1=256.0, scalar2=b1s[:, f:f + 1],
                                                op0=ALU.mult, op1=ALU.add).then_inc(ve_y1, 1)
                                v.tensor_scalar(out=x0b[:, :], in0=y1b[:, :],
                                                scalar1=-1024.0, scalar2=None, op0=ALU.max)
                                v.tensor_scalar(out=tpb[:, f % 4, :], in0=x0b[:, :],
                                                scalar1=63, scalar2=None, op0=ALU.bitwise_and)
                                if fb + f >= 4:
                                    v.wait_ge(sc_g, fb + f - 3)
                                v.tensor_scalar(out=b0b[:, f % 4, :], in0=x0b[:, :],
                                                scalar1=-64, scalar2=None,
                                                op0=ALU.bitwise_and).then_inc(ve_pw, 1)
                            if f >= 2:
                                fz = f - 2
                                v.wait_ge(gp_c, fb + fz + 1)
                                v.scalar_tensor_tensor(out=z1b[:, :], in0=tpb[:, fz % 4, :],
                                                       scalar=4.0, in1=difb[:, fz % 4, :],
                                                       op0=ALU.mult, op1=ALU.mult)
                                v.scalar_tensor_tensor(out=g2b[:, :], in0=g0b[:, fz % 2, :],
                                                       scalar=256.0, in1=z1b[:, :],
                                                       op0=ALU.mult, op1=ALU.add)
                                if fb + fz >= 4:
                                    v.wait_ge(mm_o2, fb + fz - 3)
                                v.tensor_copy(out=gbfb[:, fz % 4, :],
                                              in_=g2b[:, :]).then_inc(ve_g, 1)
                        v.wait_ge(mm_o2, NFB * (fc + 1))
                        if fc == 0:
                            v.wait_ge(ve_c, 5)
                        v.wait_ge(dx[fc % 2], 16 * (NFC // 2 + fc // 2 + 1))
                        for t in range(2):
                            if fc >= 2:
                                v.wait_ge(dout[t], 16 * (fc - 1))
                            for hh in range(2):
                                q = 2 * t + hh
                                src = (ps_a0 if q == 0 else ps_a1 if q == 1 else ps_c)
                                srcap = (src[:, 0:512] if q != 3 else src[:, 512:1024])
                                v.tensor_tensor(out=z3b[:, :], in0=srcap,
                                                in1=b2b[:, hh * 512:(hh + 1) * 512], op=ALU.add)
                                v.tensor_scalar(out=x3b[:, :], in0=z3b[:, :], scalar1=1.0,
                                                scalar2=None, op0=ALU.mult)
                                v.scalar_tensor_tensor(
                                    out=obuf[:, hh * 512:(hh + 1) * 512],
                                    in0=x3b[:, :], scalar=1.0 / 256.0,
                                    in1=xbuf[:, fc % 2, t, hh * 512:(hh + 1) * 512],
                                    op0=ALU.mult, op1=ALU.add)
                            t4 = (2 * fc + t) % 4
                            rep = bass.AP(ybuf, (i % 2) * 256 + t4 * 64,
                                          [[512, 128], [0, 16], [1, 64]])
                            v.tensor_tensor(out=of32[:, fc % 2, t, :], in0=obuf[:, :],
                                            in1=rep, op=ALU.add).then_inc(ve_asm, 1)

            def qkv_evac(s, tcq):
                s.wait_ge(mm_qkv, tcq + 1)
                if tcq == 0:
                    s.wait_ge(dg, 16 * 6)
                    s.wait_ge(ve_c, 6)
                s.activation(out=qk[:, tcq * 256:(tcq + 1) * 256], in_=ps_a0[:, 0:256],
                             func=ACT.Identity, bias=bqk_s[:, :], scale=1.0)
                s.activation(out=q2[:, tcq * 256:(tcq + 1) * 256], in_=ps_a1[:, 0:256],
                             func=ACT.Identity, bias=bq2_s[:, :], scale=1.0)
                for t in range(2):
                    col = 65 * (2 * tcq + t)
                    ev = s.activation(out=vown[:, col:col + 64], in_=ps_c[:, t * 64:(t + 1) * 64],
                                      func=ACT.Identity, bias=0.0, scale=1.0)
                ev.then_inc(sc_qk, 1)

            @block.scalar
            def _(s):
                for tc in range(p1n):
                    sl = tc % 2
                    if p1lvl < 3:
                        continue
                    s.wait_ge(ve_st, tc + 1)
                    if tc == 0:
                        s.wait_ge(ve_c, 8)
                    s.activation(out=lnb[:, sl, :], in_=varb[:, sl, :], func=ACT.Ln,
                                 bias=c_eps[:, :], scale=1.0)
                    s.activation(out=rb[:, sl, :], in_=lnb[:, sl, :], func=ACT.Exp,
                                 bias=0.0, scale=-0.5).then_inc(sc_r, 1)
                    if p1lvl < 4:
                        continue
                    s.wait_ge(ve_mb, tc + 1)
                    if tc >= 1 and p1lvl >= 5:
                        s.wait_ge(mm_tr, tc)
                    for t in range(2):
                        nr = s.activation(out=xnf[:, t, :], in_=xbuf[:, sl, t, :],
                                          func=ACT.Identity, bias=mbb[:, sl, t:t + 1],
                                          scale=rb[:, sl, t:t + 1])
                    nr.then_inc(sc_n, 1)
                    if tc >= 1 and p1lvl >= 67:
                        qkv_evac(s, tc - 1)
                if p1n >= 1 and p1lvl >= 67:
                    qkv_evac(s, p1n - 1)
                for i in range(p3n):
                    for pr in range(128):
                        gp = 128 * i + pr
                        s.wait_ge(mm_s, gp + 1)
                        if gp >= 2:
                            s.wait_ge(mm_avt, gp - 1)
                        if gp < 2:
                            s.wait_ge(dg, 16 * DG_W_DONE)
                        reg = ps_s0 if pr % 2 == 0 else ps_s1
                        s.activation(out=at[:, pr % 2, :], in_=reg[:, :], func=ACT.Exp,
                                     bias=0.0, scale=1.0 / 32.0).then_inc(exp_c, 1)
                    for half in range(2):
                        fc = 2 * i + half
                        fb = NFB * fc
                        for f in range(NFB):
                            s.wait_ge(ve_pw, fb + f + 1)
                            if fb + f == 0:
                                s.wait_ge(ve_c, 7)
                            if fb + f >= 2:
                                s.wait_ge(gp_c, fb + f - 1)
                                s.wait_ge(ve_g, fb + f - 1)
                            s.activation(out=g0b[:, f % 2, :], in_=b0b[:, f % 4, :],
                                         func=ACT.Gelu, bias=0.0, scale=1.0 / 256.0)
                            s.activation(out=g1b[:, f % 2, :], in_=b0b[:, f % 4, :],
                                         func=ACT.Gelu, bias=c_q[:, :],
                                         scale=1.0 / 256.0).then_inc(sc_g, 1)

            @block.tensor
            def _(t):
                for tc in range(p1n + 1):
                    if p1lvl < 5:
                        continue
                    if tc < p1n:
                        t.wait_ge(sc_n, tc + 1)
                        if tc == 0:
                            t.wait_ge(dg, 16 * 4)
                        if tc >= 1:
                            t.wait_ge(ve_xn, tc)
                        t.drain()
                        for ti in range(2):
                            ps = ps_s0 if ti == 0 else ps_s1
                            for cb in range(NCB):
                                tr = t.transpose(ps[:, cb * 128:(cb + 1) * 128],
                                                 xnf[:, ti, cb * 128:(cb + 1) * 128], idf[:, :])
                        tr.then_inc(mm_tr, 1)
                    if tc >= 1 and p1lvl >= 67:
                        tcq = tc - 1
                        t.wait_ge(ve_xn, tcq + 1)
                        if tcq == 0:
                            t.wait_ge(ve_c, 3)
                        if tcq >= 1:
                            t.wait_ge(sc_qk, tcq)
                        sl = tcq % 2
                        for cb in range(NCB):
                            t.matmul(ps_a0[:, 0:256], wq_bf[:, cb * 128:(cb + 1) * 128],
                                     xnt[:, sl, cb, :], start=(cb == 0), stop=(cb == NCB - 1))
                        for cb in range(NCB):
                            t.matmul(ps_a1[:, 0:256], wqq_bf[:, cb * 128:(cb + 1) * 128],
                                     xnt[:, sl, cb, :], start=(cb == 0), stop=(cb == NCB - 1))
                        for ti in range(2):
                            for cb in range(NCB):
                                vm = t.matmul(ps_c[:, ti * 64:(ti + 1) * 64],
                                              xnt[:, sl, cb, ti * 128:(ti + 1) * 128],
                                              wv_bf[:, cb * 64:(cb + 1) * 64],
                                              start=(cb == 0), stop=(cb == NCB - 1))
                        vm.then_inc(mm_qkv, 1)
                for i in range(p3n if p2 else 0):
                    t.wait_ge(dg, 16 * DG_READY)
                    if i == 0:
                        t.wait_ge(sc_qk, NFC)
                    if i >= 1:
                        t.wait_ge(ve_asm, 4 * i)
                    for pr in range(129):
                        if pr < 128:
                            gp = 128 * i + pr
                            if gp >= 2:
                                t.wait_ge(exp_c, gp - 1)
                            if pr == 1 and i >= 1:
                                t.wait_ge(ve_y1, NFB * 2 * i)
                            reg = ps_s0 if pr % 2 == 0 else ps_s1
                            t.matmul(reg[:, 0:512], k2[0:64, pr * 128:(pr + 1) * 128],
                                     q2[0:64, i * QC:(i + 1) * QC], start=True, stop=True,
                                     tile_position=(0, 0))
                            t.matmul(reg[:, 512:1024], k2[64:128, pr * 128:(pr + 1) * 128],
                                     q2[64:128, i * QC:(i + 1) * QC], start=True, stop=True,
                                     tile_position=(64, 0)).then_inc(mm_s, 1)
                        if pr >= 1:
                            pa = pr - 1
                            t.wait_ge(exp_c, 128 * i + pa + 1)
                            first = pa == 0
                            last = pa == 127
                            for ko in range(2):
                                e = 2 * pa + ko
                                acol = ko * 512
                                av = t.matmul(ps_a0[0:65, :], vaug[:, 65 * e:65 * e + 65],
                                              at[:, pa % 2, acol:acol + 512],
                                              start=(first and ko == 0),
                                              stop=(last and ko == 1))
                            av.then_inc(mm_avt, 1)
                    t.drain()
                    t.wait_ge(ve_av, i + 1)
                    for ti in range(4):
                        yt = t.transpose(ps_c[:, ti * 128:ti * 128 + 66],
                                         yts[0:66, ti * 128:(ti + 1) * 128], idf[0:66, 0:66])
                    yt.then_inc(mm_y, 1)
                    t.drain()
                    for half in range(2):
                        fc = 2 * i + half
                        fb = NFB * fc
                        for f in range(NFB + 2):
                            if f < NFB:
                                t.wait_ge(dw1[f % 4], 16 * ((fb + f) // 4 + 1))
                                t.wait_ge(dxnl[fc % 2], 16 * (fc // 2 + 1))
                                if fb + f >= 2:
                                    t.wait_ge(ve_y1, fb + f - 1)
                                if f < 2:
                                    t.wait_ge(exp_c, 128 * (i + 1))
                                off = (f % 2) * 512
                                for cb in range(NCB):
                                    hm = t.matmul(ps_s1[:, off:off + 256],
                                                  w1r[:, f % 4, cb, :], xnt[:, fc % 2, cb, :],
                                                  start=(cb == 0), stop=(cb == NCB - 1))
                                hm.then_inc(mm_h, 1)
                            if f >= 2:
                                fz = f - 2
                                t.wait_ge(ve_g, fb + fz + 1)
                                t.wait_ge(dw2, 16 * (fb + 2 * (fz // 2 + 1)))
                                if fz < 2:
                                    if half == 0:
                                        t.wait_ge(ve_y, i + 1)
                                        t.wait_ge(ve_av, i + 1)
                                    else:
                                        t.wait_ge(ve_asm, 4 * i + 2)
                                for ti in range(2):
                                    for hh in range(2):
                                        q = 2 * ti + hh
                                        dst = (ps_a0 if q == 0 else ps_a1 if q == 1 else ps_c)
                                        dstap = (dst[:, 0:512] if q != 3 else dst[:, 512:1024])
                                        om = t.matmul(dstap,
                                                      gbfb[:, fz % 4, ti * 128:(ti + 1) * 128],
                                                      w2r[:, fz % 4, hh * 512:(hh + 1) * 512],
                                                      start=(fz == 0), stop=(fz == NFB - 1))
                                om.then_inc(mm_o2, 1)
    return nc


def kernel(L0_attn_embeddings, L0_attn_Wq, L0_attn_bq, L0_attn_Wk, L0_attn_bk,
           L0_attn_Wv, L0_attn_bv, L0_ffn_w1, L0_ffn_b1, L0_ffn_w2, L0_ffn_b2):
    x = np.ascontiguousarray(np.asarray(L0_attn_embeddings, dtype=np.float32)
                             .reshape(NCORES * TOK, C))
    Wq = np.asarray(L0_attn_Wq, dtype=np.float32)
    Wk = np.asarray(L0_attn_Wk, dtype=np.float32)
    Wv = np.asarray(L0_attn_Wv, dtype=np.float32)
    bq = np.asarray(L0_attn_bq, dtype=np.float32)
    bk = np.asarray(L0_attn_bk, dtype=np.float32)
    bv = np.asarray(L0_attn_bv, dtype=np.float32)
    w1 = np.asarray(L0_ffn_w1, dtype=np.float32)
    b1 = np.asarray(L0_ffn_b1, dtype=np.float32)
    w2 = np.asarray(L0_ffn_w2, dtype=np.float32)
    b2 = np.asarray(L0_ffn_b2, dtype=np.float32)

    wqkT_ = np.ascontiguousarray(np.concatenate([Wq.T, Wk.T], axis=1))
    wqqT_ = np.ascontiguousarray(np.concatenate([Wq.T, Wq.T], axis=1))
    wvT_ = np.ascontiguousarray(Wv.T)
    bqk = np.ascontiguousarray(np.concatenate([bq, bk]))
    bq2 = np.ascontiguousarray(np.concatenate([bq, bq]))
    idf = np.eye(128, dtype=np.float32)

    nc = build_nc()
    in_maps = []
    for r in range(NCORES):
        in_maps.append({
            "x": np.ascontiguousarray(x[r * TOK:(r + 1) * TOK]),
            "wqkT": wqkT_, "wqqT": wqqT_, "wvT": wvT_,
            "bqk": bqk, "bq2": bq2, "bv": bv,
            "w1s": np.ascontiguousarray(w1[:, r * 512:(r + 1) * 512]),
            "w2s": np.ascontiguousarray(w2[r * 512:(r + 1) * 512, :]),
            "b1r": np.ascontiguousarray(b1.reshape(32, 128).T), "b2": b2, "idf": idf,
        })
    res = run_bass_kernel_spmd(nc, in_maps, core_ids=list(range(NCORES)))
    out = np.concatenate([np.asarray(r["out"]) for r in res.results], axis=0)
    return out.reshape(NCORES, TOK, C).astype(np.float32)
